# version 31
# baseline (speedup 1.0000x reference)
import numpy as np
import concourse.bass as bass
import concourse.mybir as mybir
from concourse.bass_utils import run_bass_kernel_spmd

F32 = mybir.dt.float32
BF16 = mybir.dt.bfloat16
FP8 = mybir.dt.float8e4
OP = mybir.AluOpType
AF = mybir.ActivationFunctionType
AX = mybir.AxisListType.X

B, T, D = 4, 2048, 2048
DKV, DI = 512, 6144
NSLOT = 8
TOK = NSLOT * 128
C43 = 1.5 * 2.0 ** 43
EPS_HI = 2.0 ** 30
INV_SQRT_HS = 1.0 / float(np.sqrt(np.float32(128.0), dtype=np.float32))
NEG_BIG = -1.0e30
LN_EPS = 1.0e-5
INV_D = 1.0 / 2048.0
CJ = [1, 1, 2, 2, 3, 3, 4, 4]
FFN_TC = 256
NFC = TOK // FFN_TC
NIC = DI // 128
PHASE_LIMIT = 99


class Sem:
    def __init__(self, h):
        self.h, self.n = h, 0

    def inc(self, ins, k=1):
        ins.then_inc(self.h, k)
        self.n += k


def build_program():
    nc = bass.Bass("TRN2", target_bir_lowering=False, detect_race_conditions=False)
    xb = nc.dram_tensor("xb", [T, D], F32, kind="ExternalInput").ap()
    xq = nc.dram_tensor("xq", [TOK, D], F32, kind="ExternalInput").ap()
    qidx = nc.dram_tensor("qidx", [128, NSLOT], F32, kind="ExternalInput").ap()
    WQ = nc.dram_tensor("WQ", [D, D], F32, kind="ExternalInput").ap()
    bQ = nc.dram_tensor("bQ", [1, D], F32, kind="ExternalInput").ap()
    WK = nc.dram_tensor("WK", [DKV, D], F32, kind="ExternalInput").ap()
    bK = nc.dram_tensor("bK", [1, DKV], F32, kind="ExternalInput").ap()
    WV = nc.dram_tensor("WV", [DKV, D], F32, kind="ExternalInput").ap()
    bV = nc.dram_tensor("bV", [1, DKV], F32, kind="ExternalInput").ap()
    Wproj = nc.dram_tensor("proj", [D, D], F32, kind="ExternalInput").ap()
    Wgate = nc.dram_tensor("gate_w", [D, DI], F32, kind="ExternalInput").ap()
    Wup = nc.dram_tensor("up_w", [D, DI], F32, kind="ExternalInput").ap()
    Wdown = nc.dram_tensor("down_w", [DI, D], F32, kind="ExternalInput").ap()
    gsc = nc.dram_tensor("gate_scale", [1, DI], F32, kind="ExternalInput").ap()
    usc = nc.dram_tensor("up_scale", [1, DI], F32, kind="ExternalInput").ap()
    dsc = nc.dram_tensor("down_scale", [1, D], F32, kind="ExternalInput").ap()
    out = nc.dram_tensor("out", [TOK, D], F32, kind="ExternalOutput").ap()
    sc_gate = nc.dram_tensor("sc_gate", [NIC, 128, D], BF16, kind="Internal").ap()
    sc_up = nc.dram_tensor("sc_up", [NIC, 128, D], BF16, kind="Internal").ap()
    sc_down = nc.dram_tensor("sc_down", [NIC, 128, D], BF16, kind="Internal").ap()

    xb3 = xb.rearrange("(n p) d -> n p d", p=128)
    xq3 = xq.rearrange("(s p) d -> s p d", p=128)
    out3 = out.rearrange("(s p) d -> s p d", p=128)
    WQ3 = WQ.rearrange("(n p) d -> n p d", p=128)
    WK3 = WK.rearrange("(n p) d -> n p d", p=128)
    WV3 = WV.rearrange("(n p) d -> n p d", p=128)
    Wp3 = Wproj.rearrange("(n p) d -> n p d", p=128)
    bQT = bQ.rearrange("o (c p) -> (o p) c", p=128)
    bKT = bK.rearrange("o (c p) -> (o p) c", p=128)
    gscT = gsc.rearrange("o (c p) -> (o p) c", p=128)
    uscT = usc.rearrange("o (c p) -> (o p) c", p=128)
    Wg4 = Wgate.rearrange("(k p) (i c) -> p k i c", p=128, c=128)
    Wu4 = Wup.rearrange("(k p) (i c) -> p k i c", p=128, c=128)
    Wd3 = Wdown.rearrange("(n p) d -> n p d", p=128)

    ag = nc.auto_arena()
    ident = ag.sbuf_tensor("ident", [128, 128], F32)
    qidx_t = ag.sbuf_tensor("qidx_t", [128, NSLOT], F32)
    epsr = ag.sbuf_tensor("epsr", [128, 4], F32)
    pidx = ag.sbuf_tensor("pidx", [128, 1], F32)
    ones = ag.sbuf_tensor("ones", [1, 128], F32)
    bQc = ag.sbuf_tensor("bQc", [128, 4], F32)
    bQt_s = ag.sbuf_tensor("bQt_s", [128, 16], F32)
    bKt_s = ag.sbuf_tensor("bKt_s", [128, 4], F32)
    gsc_s = ag.sbuf_tensor("gsc_s", [128, NIC], F32)
    usc_s = ag.sbuf_tensor("usc_s", [128, NIC], F32)
    red = ag.sbuf_tensor("red", [128, 64], F32)
    Zf = ag.sbuf_tensor("Zf", [128, 4 * 1024], F32)

    ab = nc.auto_arena()
    Kt = ab.sbuf_tensor("Kt", [128, 4 * 2048], F32)
    Vt = ab.sbuf_tensor("Vt", [128, 16 * 512], F32)
    WT = ab.sbuf_tensor("WT", [128, 16 * 512], F32)

    abq = nc.auto_arena()
    qw = abq.sbuf_tensor("qw", [128, 2 * 1024], F32)
    qs2 = abq.sbuf_tensor("qs2", [128, 1024], BF16)
    q8 = abq.sbuf_tensor("q8", [128, 1024], FP8)
    q8b = abq.sbuf_tensor("q8b", [128, 1024], BF16)
    qb2 = abq.sbuf_tensor("qb2", [128, 1024], BF16)
    qb4 = abq.sbuf_tensor("qb4", [128, 1024], BF16)
    qf = abq.sbuf_tensor("qf", [128, 2 * 1024], BF16)

    akv = nc.auto_arena()
    WT2 = akv.sbuf_tensor("WT2", [128, 16 * 512], F32)
    wld = akv.sbuf_tensor("wld", [128, 2048], F32)
    bVrow = akv.sbuf_tensor("bVrow", [1, DKV], F32)
    bKrow = akv.sbuf_tensor("bKrow", [1, DKV], F32)
    xtm = akv.sbuf_tensor("xtm", [128, 2048], F32)
    xfm = akv.sbuf_tensor("xfm", [128, 2 * 16 * 128], F32)
    wacc = xtm

    asc = nc.auto_arena()
    Qt = asc.sbuf_tensor("Qt", [128, 4 * 1024], F32)
    xq1 = asc.sbuf_tensor("xq1", [128, 2048], F32)
    xqf = asc.sbuf_tensor("xqf", [128, 16 * 128], F32)
    P = asc.sbuf_tensor("P", [128, 2048], F32)
    PT = asc.sbuf_tensor("PT", [128, 16 * 128], F32)
    Ztm = asc.sbuf_tensor("Ztm", [128, 512], F32)
    mask = asc.sbuf_tensor("mask", [128, 512], F32)
    miota = asc.sbuf_tensor("miota", [128, 512], F32)

    af = nc.auto_arena()
    projc = af.sbuf_tensor("projc", [128, 4 * 2048], F32)
    dscB = af.sbuf_tensor("dscB", [128, D], F32)
    x1tm = af.sbuf_tensor("x1tm", [128, 2 * 2048], F32)
    xl = af.sbuf_tensor("xl", [128, 16 * FFN_TC], BF16)
    af2 = nc.auto_arena()
    xh = af2.sbuf_tensor("xh", [128, 16 * FFN_TC], BF16)
    hh = af.sbuf_tensor("hh", [128, NIC * FFN_TC], BF16)
    hl = af.sbuf_tensor("hl", [128, NIC * FFN_TC], BF16)
    fwq = af.sbuf_tensor("fwq", [128, 4 * 2048], BF16)
    otile = af.sbuf_tensor("otile", [128, 2 * 2048], F32)
    rtile = af.sbuf_tensor("rtile", [128, 2 * 2048], F32)

    with (
        nc.psum_tensor("psS", [128, 2048], F32) as psS,
        nc.psum_tensor("psA", [128, 512], F32) as psA,
        nc.psum_tensor("psB", [128, 512], F32) as psB,
        nc.psum_tensor("psG", [128, 512], F32) as psG,
        nc.psum_tensor("psU", [128, 512], F32) as psU,
        nc.semaphore("s_ld") as h_ld,
        nc.semaphore("s_v") as h_v,
        nc.semaphore("s_a") as h_a,
        nc.semaphore("s_t") as h_t,
        nc.semaphore("s_g") as h_g,
        nc.semaphore("s_q") as h_q,
        nc.semaphore("s_w") as h_w,
        nc.semaphore("s_x") as h_x,
        nc.semaphore("s_gl") as h_gl,
        nc.semaphore("s_o") as h_o,
        nc.Block() as block,
    ):
        ld, sv, sa, st_, sg_, sq, sw, sx, sgl, so = (
            Sem(h) for h in (h_ld, h_v, h_a, h_t, h_g, h_q, h_w, h_x, h_gl, h_o))
        import contextlib
        _es = contextlib.ExitStack()
        sf = [Sem(_es.enter_context(nc.semaphore(f"s_f{i}"))) for i in range(4)]
        h_w2 = _es.enter_context(nc.semaphore("s_w2"))
        sw2 = Sem(h_w2)

        ag.allocate()
        ab.allocate()
        akv.allocate()
        abq.allocate(side="right")

        vmark = [0]
        tmark = [0]

        qunits = []
        for ic in range(NIC):
            for half in range(2):
                qunits.append(("g", ic, half))
                qunits.append(("u", ic, half))
        for ic in range(NIC):
            for half in range(2):
                qunits.append(("d", ic, half))
        NQ = len(qunits)
        qcur = [0]
        store_mark = {}

        def quant_chunk(nunits):
            for _ in range(nunits):
                u = qcur[0]
                if u >= NQ:
                    return
                kind, ic, half = qunits[u]
                s3 = u % 2
                buf = qw.ap()[:, s3 * 1024:(s3 + 1) * 1024]
                obuf = qf.ap()[:, (u % 2) * 1024:(u % 2) * 1024 + 1024]

                @block.sync
                def _(sync, kind=kind, ic=ic, half=half, buf=buf, u=u):
                    if u >= 2:
                        sync.wait_ge(h_q, u - 1)
                    if kind in ("g", "u"):
                        W4 = Wg4 if kind == "g" else Wu4
                        seg = W4[:, half * 8:half * 8 + 8, ic, :]
                        ins = sync.dma_start(
                            out=buf.rearrange("p (k c) -> p k c", c=128), in_=seg)
                    else:
                        seg = Wd3[ic, :, half * 1024:half * 1024 + 1024]
                        ins = sync.dma_start(out=buf, in_=seg)
                    sgl.inc(ins, 16)
                my_ld = sgl.n

                @block.vector
                def _(v, buf=buf, obuf=obuf, my_ld=my_ld, u=u):
                    v.wait_ge(h_gl, my_ld)
                    if u >= 2:
                        smh, smc = store_mark[u - 2]
                        v.wait_ge(smh, smc)
                    v.tensor_copy(q8.ap(), buf)
                    v.tensor_copy(q8b.ap(), q8.ap())
                    v.tensor_tensor(qs2.ap(), q8b.ap(), q8b.ap(), OP.mult)
                    v.tensor_scalar(qb2.ap(), qs2.ap(), 2.0 ** -10, 1.0, OP.is_ge, OP.add)
                    v.tensor_scalar(qb4.ap(), qs2.ap(), 2.0 ** -8, 1.0, OP.is_ge, OP.add)
                    v.tensor_tensor(qb2.ap(), qb2.ap(), qb4.ap(), OP.mult)
                    v.tensor_scalar(qs2.ap(), qs2.ap(), 2.0 ** -6, 1.0, OP.is_ge, OP.add)
                    v.tensor_tensor(qb2.ap(), qb2.ap(), qs2.ap(), OP.mult)
                    v.tensor_scalar(qb2.ap(), qb2.ap(), C43, None, OP.mult)
                    v.tensor_tensor(qs2.ap(), q8b.ap(), qb2.ap(), OP.add)
                    v.tensor_tensor(qs2.ap(), qs2.ap(), qb2.ap(), OP.subtract)
                    v.tensor_tensor(qs2.ap(), qs2.ap(), q8b.ap(), OP.subtract)
                    ins = v.scalar_tensor_tensor(obuf, qs2.ap(), epsr.ap()[:, 0:1],
                                                 q8b.ap(), OP.mult, OP.add)
                    sq.inc(ins)
                my_sq = sq.n

                @block.scalar
                def _(s, obuf=obuf, kind=kind, ic=ic, half=half, my_sq=my_sq):
                    s.wait_ge(h_q, my_sq)
                    tgt = {"g": sc_gate, "u": sc_up, "d": sc_down}[kind]
                    ins = s.dma_start(
                        out=tgt[ic, :, half * 1024:half * 1024 + 1024], in_=obuf)
                    (sw if kind in ("g", "u") else sw2).inc(ins, 16)
                store_mark[u] = ((h_w, sw.n) if kind in ("g", "u")
                                 else (h_w2, sw2.n))
                qcur[0] += 1

        @block.sync
        def _(sync):
            for dst, src in ((qidx_t, qidx), (bQt_s, bQT), (bKt_s, bKT),
                             (gsc_s, gscT), (usc_s, uscT)):
                ins = sync.dma_start(out=dst.ap(), in_=src,
                                     allow_slow_non_contiguous=True)
                ld.inc(ins, 16)
        P0_LD = ld.n

        @block.gpsimd
        def _(g):
            ins = g.iota(pidx.ap(), [[0, 1]], channel_multiplier=1,
                         allow_small_or_imprecise_dtypes=True)
            sg_.inc(ins)

        @block.vector
        def _(v):
            v.memset(epsr.ap()[:, 1:2], 1.0)
            v.memset(ones.ap(), 1.0)
            v.memset(ident.ap(), 0.0)
            ins = v.tensor_scalar(epsr.ap()[:, 0:1], epsr.ap()[:, 1:2], EPS_HI,
                                  -EPS_HI, OP.add, OP.add)
            sv.inc(ins)

        @block.gpsimd
        def _(g):
            ins = g.iota(wacc.ap()[:, 0:128], [[1, 128]], channel_multiplier=0,
                         allow_small_or_imprecise_dtypes=True)
            sg_.inc(ins)

        @block.vector
        def _(v):
            v.wait_ge(h_g, 2)
            ins = v.tensor_scalar(ident.ap(), wacc.ap()[:, 0:128], pidx.ap(),
                                  None, OP.is_equal)
            sv.inc(ins)
            v.wait_ge(h_ld, P0_LD)
            v.tensor_tensor(bQc.ap(), bQt_s.ap()[:, 0:4], bQt_s.ap()[:, 4:8], OP.add)
            v.tensor_tensor(bQc.ap(), bQc.ap(), bQt_s.ap()[:, 8:12], OP.add)
            ins = v.tensor_tensor(bQc.ap(), bQc.ap(), bQt_s.ap()[:, 12:16], OP.add)
            sv.inc(ins)
        IDENT_V = sv.n - 1
        vmark[0] = sv.n

        def transpose_2048(src_ap_fn, dst_ap_fn, dep_v, dep_ld=None, ps=psA):
            for d0 in range(0, 16, 4):
                @block.tensor
                def _(t, d0=d0):
                    t.wait_ge(h_v, max(dep_v, IDENT_V, vmark[0]))
                    if dep_ld is not None:
                        t.wait_ge(h_ld, dep_ld)
                    for i in range(4):
                        ins = t.transpose(ps.ap()[:, i * 128:(i + 1) * 128],
                                          src_ap_fn(d0 + i), ident.ap())
                    st_.inc(ins)
                tmark[0] = st_.n

                @block.vector
                def _(v, d0=d0, tg=st_.n):
                    v.wait_ge(h_t, tg)
                    for i in range(4):
                        ins = v.tensor_copy(dst_ap_fn(d0 + i),
                                            ps.ap()[:, i * 128:(i + 1) * 128])
                    sv.inc(ins)
                vmark[0] = sv.n

        def load_transpose_W(W3, nchunks, dstW=None):
            if dstW is None:
                dstW = WT
            for n in range(nchunks):
                @block.sync
                def _(sync, n=n):
                    sync.wait_ge(h_t, tmark[0])
                    ins = sync.dma_start(out=wld.ap(), in_=W3[n])
                    ld.inc(ins, 16)
                my_ld = ld.n
                transpose_2048(
                    lambda d: wld.ap()[:, d * 128:(d + 1) * 128],
                    lambda d, n=n: dstW.ap()[:, d * 512 + n * 128: d * 512 + (n + 1) * 128],
                    dep_v=0, dep_ld=my_ld, ps=psB)

        load_transpose_W(WK3, 4)

        @block.sync
        def _(sync):
            ins = sync.dma_start(out=bVrow.ap(), in_=bV)
            ld.inc(ins, 16)
            ins = sync.dma_start(out=bKrow.ap(), in_=bK)
            ld.inc(ins, 16)
        BV_LD = ld.n
        load_transpose_W(WV3, 4, dstW=WT2)

        kv_mm_mark = {}
        for ch in range(16):
            par = ch % 2
            xt_s = xtm.ap()[:, 0:2048]
            xf_s = xfm.ap()[:, par * 2048:(par + 1) * 2048]

            @block.sync
            def _(sync, ch=ch, xt_s=xt_s):
                sync.wait_ge(h_t, tmark[0])
                ins = sync.dma_start(out=xt_s, in_=xb3[ch])
                ld.inc(ins, 16)
            my_ld = ld.n
            transpose_2048(
                lambda d, xt_s=xt_s: xt_s[:, d * 128:(d + 1) * 128],
                lambda d, xf_s=xf_s: xf_s[:, d * 128:(d + 1) * 128],
                dep_v=0, dep_ld=my_ld, ps=psA if par == 0 else psB)
            XFM_V = sv.n

            @block.tensor
            def _(t, xv=XFM_V, vm=vmark[0], xf_s=xf_s):
                t.wait_ge(h_v, max(xv, vm))
                t.wait_ge(h_ld, BV_LD)
                for g in range(4):
                    for d in range(16):
                        t.matmul(psS.ap()[:, g * 512:g * 512 + 128],
                                 WT.ap()[:, d * 512 + g * 128: d * 512 + (g + 1) * 128],
                                 xf_s[:, d * 128:(d + 1) * 128],
                                 start=(d == 0), stop=False)
                    ins = t.matmul(psS.ap()[:, g * 512:g * 512 + 128],
                                   bKrow.ap()[:, g * 128:(g + 1) * 128],
                                   ones.ap()[:, 0:128],
                                   start=False, stop=True)
                st_.inc(ins)
            tmark[0] = st_.n

            @block.vector
            def _(v, ch=ch, tg=st_.n):
                v.wait_ge(h_t, tg)
                src4 = psS.ap().rearrange("p (g rest) -> p g rest", g=4)[:, :, 0:128]
                dst4 = Kt.ap().rearrange("p (g cols) -> p g cols", g=4)[
                    :, :, ch * 128:(ch + 1) * 128]
                ins = v.tensor_copy(dst4, src4)
                sv.inc(ins)
            vmark[0] = sv.n

            @block.tensor
            def _(t, xv=XFM_V, vm=vmark[0], xf_s=xf_s, par=par):
                t.wait_ge(h_v, max(xv, vm))
                t.wait_ge(h_ld, BV_LD)
                ps = psG if par == 0 else psU
                for d in range(16):
                    t.matmul(ps.ap(),
                             xf_s[:, d * 128:(d + 1) * 128],
                             WT2.ap()[:, d * 512:(d + 1) * 512],
                             start=(d == 0), stop=False)
                ins = t.matmul(ps.ap(), ones.ap()[:, 0:128], bVrow.ap(),
                               start=False, stop=True)
                st_.inc(ins)
            tmark[0] = st_.n
            kv_mm_mark[ch] = st_.n

            @block.vector
            def _(v, ch=ch, tg=st_.n, par=par):
                v.wait_ge(h_t, tg)
                ps = psG if par == 0 else psU
                ins = v.tensor_copy(Vt.ap()[:, ch * 512:(ch + 1) * 512], ps.ap())
                sv.inc(ins)
            vmark[0] = sv.n
            quant_chunk(10)
        if PHASE_LIMIT <= 2:
            quant_chunk(NQ - qcur[0])
            return nc

        for n in range(4):
            for r in range(4):
                @block.sync
                def _(sync, n=n, r=r):
                    sync.wait_ge(h_v, vmark[0])
                    ins = sync.dma_start(out=wld.ap(), in_=WQ3[n + 4 * r])
                    ld.inc(ins, 16)
                my_ld = ld.n

                @block.vector
                def _(v, r=r, my_ld=my_ld):
                    v.wait_ge(h_ld, my_ld)
                    if r == 0:
                        ins = v.tensor_copy(wacc.ap(), wld.ap())
                    else:
                        ins = v.tensor_tensor(wacc.ap(), wacc.ap(), wld.ap(), OP.add)
                    sv.inc(ins)
                vmark[0] = sv.n
            transpose_2048(
                lambda d: wacc.ap()[:, d * 128:(d + 1) * 128],
                lambda d, n=n: WT.ap()[:, d * 512 + n * 128: d * 512 + (n + 1) * 128],
                dep_v=vmark[0], ps=psB)
        akv.free()
        asc.allocate()

        for j in range(NSLOT):
            @block.sync
            def _(sync, j=j):
                sync.wait_ge(h_t, tmark[0])
                ins = sync.dma_start(out=xq1.ap(), in_=xq3[j])
                ld.inc(ins, 16)
            my_ld = ld.n
            transpose_2048(
                lambda d: xq1.ap()[:, d * 128:(d + 1) * 128],
                lambda d: xqf.ap()[:, d * 128:(d + 1) * 128],
                dep_v=0, dep_ld=my_ld, ps=psA)
            XQF_V = sv.n
            for g in range(4):
                @block.tensor
                def _(t, g=g, xv=XQF_V, vm=vmark[0]):
                    t.wait_ge(h_v, max(xv, vm))
                    for d in range(16):
                        ins = t.matmul(psB.ap()[:, 0:128],
                                       WT.ap()[:, d * 512 + g * 128: d * 512 + (g + 1) * 128],
                                       xqf.ap()[:, d * 128:(d + 1) * 128],
                                       start=(d == 0), stop=(d == 15))
                    st_.inc(ins)
                tmark[0] = st_.n

                @block.vector
                def _(v, g=g, j=j, tg=st_.n):
                    v.wait_ge(h_t, tg)
                    ins = v.tensor_scalar(
                        Qt.ap()[:, g * 1024 + j * 128: g * 1024 + (j + 1) * 128],
                        psB.ap()[:, 0:128], bQc.ap()[:, g:g + 1], None, OP.add)
                    sv.inc(ins)
                vmark[0] = sv.n
            quant_chunk(4)
        KVQ_V = sv.n
        if PHASE_LIMIT <= 3:
            quant_chunk(NQ - qcur[0])
            return nc

        for j in range(NSLOT):
            cj = CJ[j]
            kv = cj * 512

            @block.gpsimd
            def _(g, j=j, cj=cj):
                g.wait_ge(h_v, vmark[0])
                ins = g.iota(miota.ap(), [[1, 512]], base=(cj - 1) * 512,
                             channel_multiplier=0,
                             allow_small_or_imprecise_dtypes=True)
                sg_.inc(ins)
            IOTA_G = sg_.n

            @block.tensor
            def _(t, j=j, cj=cj):
                t.wait_ge(h_v, KVQ_V)
                t.wait_ge(h_x, sx.n)
                for cc in range(cj):
                    for g in range(4):
                        ins = t.matmul(
                            psS.ap()[:, cc * 512:(cc + 1) * 512],
                            Qt.ap()[:, g * 1024 + j * 128: g * 1024 + (j + 1) * 128],
                            Kt.ap()[:, g * 2048 + cc * 512: g * 2048 + (cc + 1) * 512],
                            start=(g == 0), stop=(g == 3))
                st_.inc(ins)
            S_T = st_.n

            @block.vector
            def _(v, j=j, cj=cj, kv=kv, tg=S_T, ig=IOTA_G):
                v.wait_ge(h_g, ig)
                v.tensor_scalar(mask.ap(), miota.ap(), qidx_t.ap()[:, j:j + 1],
                                NEG_BIG, OP.is_gt, OP.mult)
                v.wait_ge(h_t, tg)
                v.tensor_tensor(psS.ap()[:, (cj - 1) * 512: cj * 512],
                                psS.ap()[:, (cj - 1) * 512: cj * 512],
                                mask.ap(), OP.add)
                ins = v.reduce_max(red.ap()[:, 0:1], psS.ap()[:, 0:kv],
                                   axis=AX, negate=True)
                sv.inc(ins)
            MASK_V = sv.n
            vmark[0] = sv.n

            @block.scalar
            def _(s, kv=kv, mv=MASK_V):
                s.wait_ge(h_v, mv)
                ins = s.activation(P.ap()[:, 0:kv], psS.ap()[:, 0:kv], AF.Exp,
                                   bias=red.ap()[:, 0:1], scale=1.0,
                                   accum_out=red.ap()[:, 1:2])
                sx.inc(ins)
            EXP_X = sx.n

            @block.vector
            def _(v, ea=EXP_X):
                v.wait_ge(h_x, ea)
                ins = v.reciprocal(red.ap()[:, 2:3], red.ap()[:, 1:2])
                sv.inc(ins)
            RCP_V = sv.n

            nkb = 4 * cj
            for b0 in range(0, nkb, 4):
                nb = min(4, nkb - b0)

                @block.tensor
                def _(t, b0=b0, nb=nb, ea=EXP_X):
                    t.wait_ge(h_x, ea)
                    t.wait_ge(h_v, vmark[0])
                    for i in range(nb):
                        ins = t.transpose(psA.ap()[:, i * 128:(i + 1) * 128],
                                          P.ap()[:, (b0 + i) * 128:(b0 + i + 1) * 128],
                                          ident.ap())
                    st_.inc(ins)

                @block.vector
                def _(v, b0=b0, nb=nb, tg=st_.n):
                    v.wait_ge(h_t, tg)
                    ins = v.tensor_copy(PT.ap()[:, b0 * 128:(b0 + nb) * 128],
                                        psA.ap()[:, 0:nb * 128])
                    sv.inc(ins)
                vmark[0] = sv.n
            PT_V = sv.n

            @block.tensor
            def _(t, nkb=nkb, pv=PT_V):
                t.wait_ge(h_v, pv)
                for kb in range(nkb):
                    ins = t.matmul(psB.ap(), PT.ap()[:, kb * 128:(kb + 1) * 128],
                                   Vt.ap()[:, kb * 512:(kb + 1) * 512],
                                   start=(kb == 0), stop=(kb == nkb - 1))
                st_.inc(ins)
            Z_T = st_.n

            @block.vector
            def _(v, tg=Z_T, rv=RCP_V):
                v.wait_ge(h_t, tg)
                ins = v.tensor_scalar(Ztm.ap(), psB.ap(), red.ap()[:, 2:3],
                                      None, OP.mult)
                sv.inc(ins)
            ZTM_V = sv.n
            vmark[0] = sv.n

            @block.tensor
            def _(t, zv=ZTM_V):
                t.wait_ge(h_v, zv)
                for i in range(4):
                    ins = t.transpose(psA.ap()[:, i * 128:(i + 1) * 128],
                                      Ztm.ap()[:, i * 128:(i + 1) * 128],
                                      ident.ap())
                st_.inc(ins)

            @block.vector
            def _(v, j=j, tg=st_.n):
                v.wait_ge(h_t, tg)
                for i in range(4):
                    ins = v.tensor_copy(
                        Zf.ap()[:, i * 1024 + j * 128: i * 1024 + (j + 1) * 128],
                        psA.ap()[:, i * 128:(i + 1) * 128])
                sv.inc(ins)
            vmark[0] = sv.n
        QDONE_W = 16 * 192
        if PHASE_LIMIT <= 4:
            return nc
        asc.free()
        ab.free()
        af.allocate()
        af2.allocate(side="right")

        for g in range(4):
            for r in range(4):
                @block.sync
                def _(sync, g=g, r=r):
                    sync.wait_ge(h_v, vmark[0])
                    ins = sync.dma_start(out=otile.ap()[:, 0:2048], in_=Wp3[g + 4 * r])
                    ld.inc(ins, 16)
                my_ld = ld.n

                @block.vector
                def _(v, g=g, r=r, my_ld=my_ld):
                    v.wait_ge(h_ld, my_ld)
                    dst = projc.ap()[:, g * 2048:(g + 1) * 2048]
                    pl = otile.ap()[:, 0:2048]
                    if r == 0:
                        ins = v.tensor_copy(dst, pl)
                    elif r < 3:
                        ins = v.tensor_tensor(dst, dst, pl, OP.add)
                    else:
                        v.tensor_tensor(dst, dst, pl, OP.add)
                        ins = v.tensor_scalar(dst, dst, INV_SQRT_HS, None, OP.mult)
                    sv.inc(ins)
                vmark[0] = sv.n
        PROJC_V = sv.n
        if PHASE_LIMIT <= 5:
            return nc

        @block.sync
        def _(sync):
            ins = sync.dma_start(out=dscB.ap()[0:1, :], in_=dsc)
            ld.inc(ins, 16)
        DSC_LD = ld.n

        @block.tensor
        def _(t):
            t.wait_ge(h_v, PROJC_V)
            t.wait_ge(h_ld, DSC_LD)
            for cc, ps in enumerate((psA, psB, psG, psU)):
                ins = t.matmul(ps.ap(), ones.ap()[:, 0:128],
                               dscB.ap()[0:1, cc * 512:(cc + 1) * 512],
                               start=True, stop=True)
            st_.inc(ins)

        @block.vector
        def _(v, base_t=st_.n):
            v.wait_ge(h_t, base_t)
            for cc, ps in enumerate((psA, psB, psG, psU)):
                ins = v.tensor_copy(dscB.ap()[:, cc * 512:(cc + 1) * 512], ps.ap())
                sv.inc(ins)
        vmark[0] = sv.n
        tmark[0] = st_.n
        psfree = [sv.n]
        if PHASE_LIMIT <= 5.5:
            return nc

        for c in range(NFC):
            if PHASE_LIMIT <= 5.6 and c > 0:
                return nc
            @block.sync
            def _(sync, c=c):
                sync.wait_ge(h_v, vmark[0])
                for jj in range(2):
                    ins = sync.dma_start(out=otile.ap()[:, jj * 2048:(jj + 1) * 2048],
                                         in_=xq3[2 * c + jj])
                    ld.inc(ins, 16)
            XQ_LD = ld.n

            ao_marks = []
            ao_gate = [0]
            for jj in range(2):
                @block.tensor
                def _(t, c=c, jj=jj, pf=None):
                    t.wait_ge(h_v, max(PROJC_V, psfree[0] if c > 0 else vmark[0],
                                       ao_gate[0]))
                    for dd in range(4):
                        for g in range(4):
                            ins = t.matmul(
                                psS.ap()[:, dd * 512:(dd + 1) * 512],
                                Zf.ap()[:, g * 1024 + (2 * c + jj) * 128:
                                        g * 1024 + (2 * c + jj + 1) * 128],
                                projc.ap()[:, g * 2048 + dd * 512:
                                           g * 2048 + (dd + 1) * 512],
                                start=(g == 0), stop=(g == 3))
                    st_.inc(ins)
                tmark[0] = st_.n

                @block.vector
                def _(v, jj=jj, tg=st_.n, xl_=XQ_LD, oo=so.n):
                    v.wait_ge(h_t, tg)
                    v.wait_ge(h_ld, xl_)
                    v.wait_ge(h_o, oo)
                    ins = v.scalar_tensor_tensor(
                        rtile.ap()[:, jj * 2048:(jj + 1) * 2048],
                        psS.ap(), 1.0, otile.ap()[:, jj * 2048:(jj + 1) * 2048],
                        OP.mult, OP.add,
                        accum_out=red.ap()[:, 32 + jj * 8:33 + jj * 8])
                    sv.inc(ins)
                ao_marks.append(sv.n)
                ao_gate[0] = sv.n
                vmark[0] = sv.n

            @block.vector
            def _(v):
                for jj in range(2):
                    v.tensor_scalar(red.ap()[:, 33 + jj * 8:34 + jj * 8],
                                    red.ap()[:, 32 + jj * 8:33 + jj * 8],
                                    INV_D, None, OP.mult)
                for jj in range(2):
                    v.tensor_scalar(otile.ap()[:, jj * 2048:(jj + 1) * 2048],
                                    rtile.ap()[:, jj * 2048:(jj + 1) * 2048],
                                    red.ap()[:, 33 + jj * 8:34 + jj * 8],
                                    None, OP.subtract)
                for jj in range(2):
                    v.scalar_tensor_tensor(
                        rtile.ap()[:, jj * 2048:(jj + 1) * 2048],
                        otile.ap()[:, jj * 2048:(jj + 1) * 2048], 1.0,
                        otile.ap()[:, jj * 2048:(jj + 1) * 2048], OP.mult, OP.mult,
                        accum_out=red.ap()[:, 34 + jj * 8:35 + jj * 8])
                for jj in range(2):
                    ins = v.tensor_scalar(red.ap()[:, 35 + jj * 8:36 + jj * 8],
                                          red.ap()[:, 34 + jj * 8:35 + jj * 8],
                                          INV_D, LN_EPS, OP.mult, OP.add)
                sv.inc(ins)
            VAR_V = sv.n

            @block.scalar
            def _(s, vv=VAR_V):
                s.wait_ge(h_v, vv)
                for jj in range(2):
                    ins = s.activation(red.ap()[:, 36 + jj * 8:37 + jj * 8],
                                       red.ap()[:, 35 + jj * 8:36 + jj * 8], AF.Sqrt)
                sa.inc(ins)
            SQ_A = sa.n

            @block.vector
            def _(v, qa=SQ_A):
                v.wait_ge(h_a, qa)
                for jj in range(2):
                    v.reciprocal(red.ap()[:, 37 + jj * 8:38 + jj * 8],
                                 red.ap()[:, 36 + jj * 8:37 + jj * 8])
                for jj in range(2):
                    v.tensor_tensor(red.ap()[:, 38 + jj * 8:39 + jj * 8],
                                    red.ap()[:, 37 + jj * 8:38 + jj * 8],
                                    red.ap()[:, 37 + jj * 8:38 + jj * 8], OP.mult)
                for jj in range(2):
                    v.tensor_tensor(red.ap()[:, 38 + jj * 8:39 + jj * 8],
                                    red.ap()[:, 38 + jj * 8:39 + jj * 8],
                                    red.ap()[:, 35 + jj * 8:36 + jj * 8], OP.mult)
                for jj in range(2):
                    v.tensor_scalar(red.ap()[:, 38 + jj * 8:39 + jj * 8],
                                    red.ap()[:, 38 + jj * 8:39 + jj * 8],
                                    -0.5, 1.5, OP.mult, OP.add)
                for jj in range(2):
                    v.tensor_tensor(red.ap()[:, 37 + jj * 8:38 + jj * 8],
                                    red.ap()[:, 37 + jj * 8:38 + jj * 8],
                                    red.ap()[:, 38 + jj * 8:39 + jj * 8], OP.mult)
                for jj in range(2):
                    ins = v.tensor_scalar(
                        x1tm.ap()[:, jj * 2048:(jj + 1) * 2048],
                        otile.ap()[:, jj * 2048:(jj + 1) * 2048],
                        red.ap()[:, 37 + jj * 8:38 + jj * 8], None, OP.mult)
                    sv.inc(ins)
            X1_V = sv.n
            vmark[0] = sv.n
            if PHASE_LIMIT <= 6:
                return nc

            for jj in range(2):
                for d0 in range(0, 16, 4):
                    @block.tensor
                    def _(t, jj=jj, d0=d0, xv=X1_V):
                        t.wait_ge(h_v, max(xv, vmark[0]))
                        for i in range(4):
                            ins = t.transpose(
                                psA.ap()[:, i * 128:(i + 1) * 128],
                                x1tm.ap()[:, jj * 2048 + (d0 + i) * 128:
                                          jj * 2048 + (d0 + i + 1) * 128],
                                ident.ap())
                        st_.inc(ins)
                    tmark[0] = st_.n

                    @block.vector
                    def _(v, jj=jj, d0=d0, tg=st_.n):
                        v.wait_ge(h_t, tg)
                        for i in range(4):
                            dsl = slice((d0 + i) * 256 + jj * 128,
                                        (d0 + i) * 256 + (jj + 1) * 128)
                            v.tensor_copy(xh.ap()[:, dsl],
                                          psA.ap()[:, i * 128:(i + 1) * 128])
                            ins = v.scalar_tensor_tensor(
                                xl.ap()[:, dsl], psA.ap()[:, i * 128:(i + 1) * 128],
                                1.0, xh.ap()[:, dsl], OP.mult, OP.subtract)
                        sv.inc(ins)
                    vmark[0] = sv.n
            XHL_V = sv.n
            if PHASE_LIMIT <= 6.5:
                return nc

            ic_te_mark = {}
            ic_hl_mark = {}
            for ic in range(NIC):
                s2 = ic % 2
                gbuf = fwq.ap()[:, (2 * s2) * 2048:(2 * s2 + 1) * 2048]
                ubuf = fwq.ap()[:, (2 * s2 + 1) * 2048:(2 * s2 + 2) * 2048]

                gslot, uslot = 2 * s2, 2 * s2 + 1

                @block.sync
                def _(sync, c=c, ic=ic, gbuf=gbuf, ubuf=ubuf, gs=gslot, us=uslot):
                    if ic >= 2:
                        sync.wait_ge(h_t, ic_te_mark[("g", ic - 2)])
                    sync.wait_ge(h_w, min(16 * (4 * ic + 4), QDONE_W))
                    ins = sync.dma_start(out=gbuf[:, 0:2048], in_=sc_gate[ic])
                    sf[gs].inc(ins, 16)
                    if ic >= 2:
                        sync.wait_ge(h_t, ic_te_mark[("u", ic - 2)])
                    ins = sync.dma_start(out=ubuf[:, 0:2048], in_=sc_up[ic])
                    sf[us].inc(ins, 16)
                GLS, GLC = sf[gslot].h, sf[gslot].n
                ULS, ULC = sf[uslot].h, sf[uslot].n

                @block.tensor
                def _(t, ic=ic, gbuf=gbuf, gls=GLS, glc=GLC, xv=XHL_V):
                    t.wait_ge(h_v, xv)
                    t.wait_ge(gls, glc)
                    if ic >= 1:
                        t.wait_ge(h_a, ic_te_mark.get(("silu", ic - 1), 0))
                    for d in range(16):
                        for hsel, rhs in ((0, xh), (1, xl)):
                            ins = t.matmul(psG.ap()[:, 0:FFN_TC],
                                           gbuf[:, d * 128:(d + 1) * 128],
                                           rhs.ap()[:, d * FFN_TC:(d + 1) * FFN_TC],
                                           start=(d == 0 and hsel == 0),
                                           stop=(d == 15 and hsel == 1))
                    st_.inc(ins)
                G_T = st_.n
                ic_te_mark[("g", ic)] = G_T

                @block.scalar
                def _(s, ic=ic, tg=G_T, pv=ic_hl_mark.get(ic - 1, 0)):
                    s.wait_ge(h_t, tg)
                    s.wait_ge(h_v, pv)
                    ins = s.activation(Zf.ap()[:, 0:FFN_TC], psG.ap()[:, 0:FFN_TC],
                                       AF.Silu, bias=0.0,
                                       scale=gsc_s.ap()[:, ic:ic + 1])
                    sa.inc(ins)
                ic_te_mark[("silu", ic)] = sa.n
                SIL_A = sa.n

                @block.tensor
                def _(t, ic=ic, ubuf=ubuf, uls=ULS, ulc=ULC,
                      hv=ic_hl_mark.get(ic - 1, 0)):
                    t.wait_ge(uls, ulc)
                    t.wait_ge(h_v, hv)
                    for d in range(16):
                        for hsel, rhs in ((0, xh), (1, xl)):
                            ins = t.matmul(psU.ap()[:, 0:FFN_TC],
                                           ubuf[:, d * 128:(d + 1) * 128],
                                           rhs.ap()[:, d * FFN_TC:(d + 1) * FFN_TC],
                                           start=(d == 0 and hsel == 0),
                                           stop=(d == 15 and hsel == 1))
                    st_.inc(ins)
                ic_te_mark[("u", ic)] = st_.n
                U_T = st_.n
                tmark[0] = st_.n
                quant_chunk(2)

                @block.vector
                def _(v, ic=ic, ut=U_T, sa_=SIL_A):
                    v.wait_ge(h_t, ut)
                    v.wait_ge(h_a, sa_)
                    v.scalar_tensor_tensor(psB.ap()[:, 0:FFN_TC], psU.ap()[:, 0:FFN_TC],
                                           usc_s.ap()[:, ic:ic + 1], Zf.ap()[:, 0:FFN_TC],
                                           OP.mult, OP.mult)
                    v.tensor_copy(hh.ap()[:, ic * FFN_TC:(ic + 1) * FFN_TC],
                                  psB.ap()[:, 0:FFN_TC])
                    ins = v.scalar_tensor_tensor(
                        hl.ap()[:, ic * FFN_TC:(ic + 1) * FFN_TC], psB.ap()[:, 0:FFN_TC],
                        1.0, hh.ap()[:, ic * FFN_TC:(ic + 1) * FFN_TC], OP.mult, OP.subtract)
                    sv.inc(ins)
                ic_hl_mark[ic] = sv.n
                vmark[0] = sv.n
            HDONE_V = sv.n
            if PHASE_LIMIT <= 7:
                return nc
            if PHASE_LIMIT <= 7.2 and c == 0:
                return nc

            dbanks = [[(psS, cc * 512) for cc in range(4)],
                      [(psG, 0), (psU, 0), (psA, 0), (psB, 0)]]
            dn_te = {}
            for ic in range(NIC):
                s2 = ic % 4
                buf = fwq.ap()[:, s2 * 2048:(s2 + 1) * 2048]

                @block.sync
                def _(sync, c=c, ic=ic, buf=buf, ds=s2):
                    if ic < 4:
                        sync.wait_ge(h_t, tmark[0])
                    else:
                        sync.wait_ge(h_t, dn_te[ic - 4])
                    sync.wait_ge(h_w2, 16 * 96)
                    ins = sync.dma_start(out=buf[:, 0:2048], in_=sc_down[ic])
                    sf[ds].inc(ins, 16)
                DLS, DLC = sf[s2].h, sf[s2].n

                @block.tensor
                def _(t, ic=ic, buf=buf, dls=DLS, dlc=DLC,
                      hv=(HDONE_V if ic == 0 else ic_hl_mark[ic]),
                      la=(ic_te_mark[("silu", NIC - 1)] if ic == 0 else 0)):
                    t.wait_ge(dls, dlc)
                    t.wait_ge(h_v, hv)
                    if la:
                        t.wait_ge(h_a, la)
                    for s in range(2):
                        for hsel, lhs in ((0, hh), (1, hl)):
                            for cc in range(4):
                                if s == 0:
                                    tgt = psS.ap()[:, cc * 512:(cc + 1) * 512]
                                else:
                                    tgt = dbanks[1][cc][0].ap()
                                ins = t.matmul(
                                    tgt,
                                    lhs.ap()[:, ic * FFN_TC + s * 128:
                                             ic * FFN_TC + (s + 1) * 128],
                                    buf[:, cc * 512:(cc + 1) * 512],
                                    start=(ic == 0 and hsel == 0),
                                    stop=(ic == NIC - 1 and hsel == 1))
                    st_.inc(ins)
                dn_te[ic] = st_.n
                tmark[0] = st_.n
            DN_T = st_.n
            if PHASE_LIMIT <= 7.5:
                return nc

            @block.vector
            def _(v, c=c, tg=DN_T):
                v.wait_ge(h_t, tg)
                for s in range(2):
                    for cc in range(4):
                        ps, off = dbanks[s][cc]
                        src = (psS.ap()[:, cc * 512:(cc + 1) * 512] if s == 0
                               else ps.ap())
                        ins = v.tensor_tensor(
                            otile.ap()[:, s * 2048 + cc * 512: s * 2048 + (cc + 1) * 512],
                            src, dscB.ap()[:, cc * 512:(cc + 1) * 512], OP.mult)
                sv.inc(ins)
            psfree[0] = sv.n

            @block.vector
            def _(v, c=c):
                for s in range(2):
                    v.scalar_tensor_tensor(
                        rtile.ap()[:, s * 2048:(s + 1) * 2048],
                        otile.ap()[:, s * 2048:(s + 1) * 2048], 1.0,
                        x1tm.ap()[:, s * 2048:(s + 1) * 2048], OP.mult, OP.add,
                        accum_out=red.ap()[:, 48 + s * 8:49 + s * 8])
                for s in range(2):
                    v.tensor_scalar(red.ap()[:, 49 + s * 8:50 + s * 8],
                                    red.ap()[:, 48 + s * 8:49 + s * 8],
                                    INV_D, None, OP.mult)
                for s in range(2):
                    v.tensor_scalar(otile.ap()[:, s * 2048:(s + 1) * 2048],
                                    rtile.ap()[:, s * 2048:(s + 1) * 2048],
                                    red.ap()[:, 49 + s * 8:50 + s * 8],
                                    None, OP.subtract)
                for s in range(2):
                    v.scalar_tensor_tensor(
                        rtile.ap()[:, s * 2048:(s + 1) * 2048],
                        otile.ap()[:, s * 2048:(s + 1) * 2048], 1.0,
                        otile.ap()[:, s * 2048:(s + 1) * 2048], OP.mult, OP.mult,
                        accum_out=red.ap()[:, 50 + s * 8:51 + s * 8])
                for s in range(2):
                    ins = v.tensor_scalar(red.ap()[:, 51 + s * 8:52 + s * 8],
                                          red.ap()[:, 50 + s * 8:51 + s * 8],
                                          INV_D, LN_EPS, OP.mult, OP.add)
                sv.inc(ins)
            VAR2_V = sv.n

            @block.scalar
            def _(s_, vv=VAR2_V):
                s_.wait_ge(h_v, vv)
                for s in range(2):
                    ins = s_.activation(red.ap()[:, 52 + s * 8:53 + s * 8],
                                        red.ap()[:, 51 + s * 8:52 + s * 8], AF.Sqrt)
                sa.inc(ins)
            SQ2_A = sa.n

            @block.vector
            def _(v, qa=SQ2_A):
                v.wait_ge(h_a, qa)
                for s in range(2):
                    v.reciprocal(red.ap()[:, 53 + s * 8:54 + s * 8],
                                 red.ap()[:, 52 + s * 8:53 + s * 8])
                for s in range(2):
                    v.tensor_tensor(red.ap()[:, 54 + s * 8:55 + s * 8],
                                    red.ap()[:, 53 + s * 8:54 + s * 8],
                                    red.ap()[:, 53 + s * 8:54 + s * 8], OP.mult)
                for s in range(2):
                    v.tensor_tensor(red.ap()[:, 54 + s * 8:55 + s * 8],
                                    red.ap()[:, 54 + s * 8:55 + s * 8],
                                    red.ap()[:, 51 + s * 8:52 + s * 8], OP.mult)
                for s in range(2):
                    v.tensor_scalar(red.ap()[:, 54 + s * 8:55 + s * 8],
                                    red.ap()[:, 54 + s * 8:55 + s * 8],
                                    -0.5, 1.5, OP.mult, OP.add)
                for s in range(2):
                    v.tensor_tensor(red.ap()[:, 53 + s * 8:54 + s * 8],
                                    red.ap()[:, 53 + s * 8:54 + s * 8],
                                    red.ap()[:, 54 + s * 8:55 + s * 8], OP.mult)
                for s in range(2):
                    ins = v.tensor_scalar(
                        rtile.ap()[:, s * 2048:(s + 1) * 2048],
                        otile.ap()[:, s * 2048:(s + 1) * 2048],
                        red.ap()[:, 53 + s * 8:54 + s * 8], None, OP.mult)
                    sv.inc(ins)
            OUT_V = sv.n
            vmark[0] = sv.n

            @block.scalar
            def _(s, c=c, ov=OUT_V):
                s.wait_ge(h_v, ov)
                for jj in range(2):
                    ins = s.dma_start(out=out3[2 * c + jj],
                                      in_=rtile.ap()[:, jj * 2048:(jj + 1) * 2048])
                    so.inc(ins, 16)

        af.free()
        af2.free()
        abq.free()
        ag.free()

        @block.sync
        def _(sync):
            sync.wait_ge(h_o, so.n)
    return nc


_PROGRAM_CACHE = {}


def kernel(L0_attn_x, L0_attn_WQ, L0_attn_bQ, L0_attn_WK, L0_attn_bK,
           L0_attn_WV, L0_attn_bV, L0_attn_proj, L0_ffn_gate_w, L0_ffn_up_w,
           L0_ffn_down_w, L0_ffn_gate_scale, L0_ffn_up_scale,
           L0_ffn_down_scale):
    x = np.ascontiguousarray(np.asarray(L0_attn_x, dtype=np.float32))
    f32 = lambda a: np.ascontiguousarray(np.asarray(a, dtype=np.float32))
    WQv, bQv = f32(L0_attn_WQ), f32(L0_attn_bQ).reshape(1, D)
    WKv, bKv = f32(L0_attn_WK), f32(L0_attn_bK).reshape(1, DKV)
    WVv, bVv = f32(L0_attn_WV), f32(L0_attn_bV).reshape(1, DKV)
    projv = f32(L0_attn_proj)
    gwv, uwv, dwv = f32(L0_ffn_gate_w), f32(L0_ffn_up_w), f32(L0_ffn_down_w)
    gsv = f32(L0_ffn_gate_scale).reshape(1, DI)
    usv = f32(L0_ffn_up_scale).reshape(1, DI)
    dsv = f32(L0_ffn_down_scale).reshape(1, D)

    if "nc" not in _PROGRAM_CACHE:
        _PROGRAM_CACHE["nc"] = build_program()
    nc = _PROGRAM_CACHE["nc"]

    in_maps = []
    p = np.arange(128, dtype=np.float32)
    for c in range(8):
        b, h = c // 2, c % 2
        blocks = [2 * j + h for j in range(NSLOT)]
        xqv = np.ascontiguousarray(
            x[b].reshape(16, 128, D)[blocks].reshape(TOK, D))
        qidxv = np.stack([(2 * j + h) * 128 + p for j in range(NSLOT)],
                         axis=1).astype(np.float32)
        in_maps.append({
            "xb": np.ascontiguousarray(x[b]), "xq": xqv, "qidx": qidxv,
            "WQ": WQv, "bQ": bQv, "WK": WKv, "bK": bKv, "WV": WVv, "bV": bVv,
            "proj": projv, "gate_w": gwv, "up_w": uwv, "down_w": dwv,
            "gate_scale": gsv, "up_scale": usv, "down_scale": dsv,
        })
    res = run_bass_kernel_spmd(nc, in_maps, core_ids=list(range(8)))
    y = np.empty((B, T, D), dtype=np.float32)
    for c in range(8):
        b, h = c // 2, c % 2
        o = res.results[c]["out"].reshape(NSLOT, 128, D)
        for j in range(NSLOT):
            y[b, (2 * j + h) * 128:(2 * j + h + 1) * 128, :] = o[j]
    return y


# version 32
# speedup vs baseline: 1.0383x; 1.0383x over previous
import numpy as np
import concourse.bass as bass
import concourse.mybir as mybir
from concourse.bass_utils import run_bass_kernel_spmd

F32 = mybir.dt.float32
BF16 = mybir.dt.bfloat16
FP8 = mybir.dt.float8e4
OP = mybir.AluOpType
AF = mybir.ActivationFunctionType
AX = mybir.AxisListType.X

B, T, D = 4, 2048, 2048
DKV, DI = 512, 6144
NSLOT = 8
TOK = NSLOT * 128
C43 = 1.5 * 2.0 ** 43
EPS_HI = 2.0 ** 30
INV_SQRT_HS = 1.0 / float(np.sqrt(np.float32(128.0), dtype=np.float32))
NEG_BIG = -1.0e30
LN_EPS = 1.0e-5
INV_D = 1.0 / 2048.0
CJ = [1, 1, 2, 2, 3, 3, 4, 4]
FFN_TC = 256
NFC = TOK // FFN_TC
NIC = DI // 128
PHASE_LIMIT = 99


class Sem:
    def __init__(self, h):
        self.h, self.n = h, 0

    def inc(self, ins, k=1):
        ins.then_inc(self.h, k)
        self.n += k


def build_program():
    nc = bass.Bass("TRN2", target_bir_lowering=False, detect_race_conditions=False)
    xb = nc.dram_tensor("xb", [T, D], F32, kind="ExternalInput").ap()
    xq = nc.dram_tensor("xq", [TOK, D], F32, kind="ExternalInput").ap()
    qidx = nc.dram_tensor("qidx", [128, NSLOT], F32, kind="ExternalInput").ap()
    WQ = nc.dram_tensor("WQ", [D, D], F32, kind="ExternalInput").ap()
    bQ = nc.dram_tensor("bQ", [1, D], F32, kind="ExternalInput").ap()
    WK = nc.dram_tensor("WK", [DKV, D], F32, kind="ExternalInput").ap()
    bK = nc.dram_tensor("bK", [1, DKV], F32, kind="ExternalInput").ap()
    WV = nc.dram_tensor("WV", [DKV, D], F32, kind="ExternalInput").ap()
    bV = nc.dram_tensor("bV", [1, DKV], F32, kind="ExternalInput").ap()
    Wproj = nc.dram_tensor("proj", [D, D], F32, kind="ExternalInput").ap()
    Wgate = nc.dram_tensor("gate_w", [D, DI], F32, kind="ExternalInput").ap()
    Wup = nc.dram_tensor("up_w", [D, DI], F32, kind="ExternalInput").ap()
    Wdown = nc.dram_tensor("down_w", [DI, D], F32, kind="ExternalInput").ap()
    gsc = nc.dram_tensor("gate_scale", [1, DI], F32, kind="ExternalInput").ap()
    usc = nc.dram_tensor("up_scale", [1, DI], F32, kind="ExternalInput").ap()
    dsc = nc.dram_tensor("down_scale", [1, D], F32, kind="ExternalInput").ap()
    out = nc.dram_tensor("out", [TOK, D], F32, kind="ExternalOutput").ap()
    sc_gate = nc.dram_tensor("sc_gate", [NIC, 128, D], BF16, kind="Internal").ap()
    sc_up = nc.dram_tensor("sc_up", [NIC, 128, D], BF16, kind="Internal").ap()
    sc_down = nc.dram_tensor("sc_down", [NIC, 128, D], BF16, kind="Internal").ap()

    xb3 = xb.rearrange("(n p) d -> n p d", p=128)
    xq3 = xq.rearrange("(s p) d -> s p d", p=128)
    out3 = out.rearrange("(s p) d -> s p d", p=128)
    WQ3 = WQ.rearrange("(n p) d -> n p d", p=128)
    WK3 = WK.rearrange("(n p) d -> n p d", p=128)
    WV3 = WV.rearrange("(n p) d -> n p d", p=128)
    Wp3 = Wproj.rearrange("(n p) d -> n p d", p=128)
    bQT = bQ.rearrange("o (c p) -> (o p) c", p=128)
    bKT = bK.rearrange("o (c p) -> (o p) c", p=128)
    gscT = gsc.rearrange("o (c p) -> (o p) c", p=128)
    uscT = usc.rearrange("o (c p) -> (o p) c", p=128)
    Wg4 = Wgate.rearrange("(k p) (i c) -> p k i c", p=128, c=128)
    Wu4 = Wup.rearrange("(k p) (i c) -> p k i c", p=128, c=128)
    Wd3 = Wdown.rearrange("(n p) d -> n p d", p=128)

    ag = nc.auto_arena()
    ident = ag.sbuf_tensor("ident", [128, 128], F32)
    qidx_t = ag.sbuf_tensor("qidx_t", [128, NSLOT], F32)
    epsr = ag.sbuf_tensor("epsr", [128, 4], F32)
    pidx = ag.sbuf_tensor("pidx", [128, 1], F32)
    ones = ag.sbuf_tensor("ones", [1, 128], F32)
    bQc = ag.sbuf_tensor("bQc", [128, 4], F32)
    bQt_s = ag.sbuf_tensor("bQt_s", [128, 16], F32)
    bKt_s = ag.sbuf_tensor("bKt_s", [128, 4], F32)
    gsc_s = ag.sbuf_tensor("gsc_s", [128, NIC], F32)
    usc_s = ag.sbuf_tensor("usc_s", [128, NIC], F32)
    red = ag.sbuf_tensor("red", [128, 64], F32)
    Zf = ag.sbuf_tensor("Zf", [128, 4 * 1024], F32)

    ab = nc.auto_arena()
    Kt = ab.sbuf_tensor("Kt", [128, 4 * 2048], F32)
    Vt = ab.sbuf_tensor("Vt", [128, 16 * 512], F32)
    WT = ab.sbuf_tensor("WT", [128, 16 * 512], F32)

    abq = nc.auto_arena()
    qw = abq.sbuf_tensor("qw", [128, 2 * 1024], F32)
    qs2 = abq.sbuf_tensor("qs2", [128, 1024], BF16)
    q8 = abq.sbuf_tensor("q8", [128, 1024], FP8)
    q8b = abq.sbuf_tensor("q8b", [128, 1024], BF16)
    qb2 = abq.sbuf_tensor("qb2", [128, 1024], BF16)
    qb4 = abq.sbuf_tensor("qb4", [128, 1024], BF16)
    qf = abq.sbuf_tensor("qf", [128, 2 * 1024], BF16)

    akv = nc.auto_arena()
    WT2 = akv.sbuf_tensor("WT2", [128, 16 * 512], F32)
    wld = akv.sbuf_tensor("wld", [128, 2048], F32)
    bVrow = akv.sbuf_tensor("bVrow", [1, DKV], F32)
    bKrow = akv.sbuf_tensor("bKrow", [1, DKV], F32)
    xtm = akv.sbuf_tensor("xtm", [128, 2048], F32)
    xfm = akv.sbuf_tensor("xfm", [128, 2 * 16 * 128], F32)
    wacc = xtm

    asc = nc.auto_arena()
    Qt = asc.sbuf_tensor("Qt", [128, 4 * 1024], F32)
    xq1 = asc.sbuf_tensor("xq1", [128, 2048], F32)
    xqf = asc.sbuf_tensor("xqf", [128, 16 * 128], F32)
    P = asc.sbuf_tensor("P", [128, 2048], F32)
    PT = asc.sbuf_tensor("PT", [128, 16 * 128], F32)
    Ztm = asc.sbuf_tensor("Ztm", [128, 512], F32)
    mask = asc.sbuf_tensor("mask", [128, 512], F32)
    miota = asc.sbuf_tensor("miota", [128, 512], F32)

    af = nc.auto_arena()
    projc = af.sbuf_tensor("projc", [128, 4 * 2048], F32)
    dscB = af.sbuf_tensor("dscB", [128, D], F32)
    x1tm = af.sbuf_tensor("x1tm", [128, 2 * 2048], F32)
    xl = af.sbuf_tensor("xl", [128, 16 * FFN_TC], BF16)
    af2 = nc.auto_arena()
    xh = af2.sbuf_tensor("xh", [128, 16 * FFN_TC], BF16)
    hh = af.sbuf_tensor("hh", [128, NIC * FFN_TC], BF16)
    hl = af.sbuf_tensor("hl", [128, NIC * FFN_TC], BF16)
    fwq = af.sbuf_tensor("fwq", [128, 4 * 2048], BF16)
    otile = af.sbuf_tensor("otile", [128, 2 * 2048], F32)
    rtile = af.sbuf_tensor("rtile", [128, 2 * 2048], F32)

    with (
        nc.psum_tensor("psS", [128, 2048], F32) as psS,
        nc.psum_tensor("psA", [128, 512], F32) as psA,
        nc.psum_tensor("psB", [128, 512], F32) as psB,
        nc.psum_tensor("psG", [128, 512], F32) as psG,
        nc.psum_tensor("psU", [128, 512], F32) as psU,
        nc.semaphore("s_ld") as h_ld,
        nc.semaphore("s_v") as h_v,
        nc.semaphore("s_a") as h_a,
        nc.semaphore("s_t") as h_t,
        nc.semaphore("s_g") as h_g,
        nc.semaphore("s_q") as h_q,
        nc.semaphore("s_w") as h_w,
        nc.semaphore("s_x") as h_x,
        nc.semaphore("s_gl") as h_gl,
        nc.semaphore("s_o") as h_o,
        nc.Block() as block,
    ):
        ld, sv, sa, st_, sg_, sq, sw, sx, sgl, so = (
            Sem(h) for h in (h_ld, h_v, h_a, h_t, h_g, h_q, h_w, h_x, h_gl, h_o))
        import contextlib
        _es = contextlib.ExitStack()
        sf = [Sem(_es.enter_context(nc.semaphore(f"s_f{i}"))) for i in range(4)]
        h_w2 = _es.enter_context(nc.semaphore("s_w2"))
        sw2 = Sem(h_w2)

        ag.allocate()
        ab.allocate()
        akv.allocate()
        abq.allocate(side="right")

        vmark = [0]
        tmark = [0]

        qunits = []
        for ic in range(NIC):
            for half in range(2):
                qunits.append(("g", ic, half))
                qunits.append(("u", ic, half))
        for ic in range(NIC):
            for half in range(2):
                qunits.append(("d", ic, half))
        NQ = len(qunits)
        qcur = [0]
        store_mark = {}

        def quant_chunk(nunits):
            for _ in range(nunits):
                u = qcur[0]
                if u >= NQ:
                    return
                kind, ic, half = qunits[u]
                s3 = u % 2
                buf = qw.ap()[:, s3 * 1024:(s3 + 1) * 1024]
                obuf = qf.ap()[:, (u % 2) * 1024:(u % 2) * 1024 + 1024]

                @block.sync
                def _(sync, kind=kind, ic=ic, half=half, buf=buf, u=u):
                    if u >= 2:
                        sync.wait_ge(h_q, u - 1)
                    if kind in ("g", "u"):
                        W4 = Wg4 if kind == "g" else Wu4
                        seg = W4[:, half * 8:half * 8 + 8, ic, :]
                        ins = sync.dma_start(
                            out=buf.rearrange("p (k c) -> p k c", c=128), in_=seg)
                    else:
                        seg = Wd3[ic, :, half * 1024:half * 1024 + 1024]
                        ins = sync.dma_start(out=buf, in_=seg)
                    sgl.inc(ins, 16)
                my_ld = sgl.n

                @block.vector
                def _(v, buf=buf, obuf=obuf, my_ld=my_ld, u=u):
                    v.wait_ge(h_gl, my_ld)
                    if u >= 2:
                        smh, smc = store_mark[u - 2]
                        v.wait_ge(smh, smc)
                    v.tensor_copy(q8.ap(), buf)
                    v.tensor_copy(q8b.ap(), q8.ap())
                    v.tensor_tensor(qs2.ap(), q8b.ap(), q8b.ap(), OP.mult)
                    v.tensor_scalar(qb2.ap(), qs2.ap(), 2.0 ** -10, 1.0, OP.is_ge, OP.add)
                    v.tensor_scalar(qb4.ap(), qs2.ap(), 2.0 ** -8, 1.0, OP.is_ge, OP.add)
                    v.tensor_tensor(qb2.ap(), qb2.ap(), qb4.ap(), OP.mult)
                    v.tensor_scalar(qs2.ap(), qs2.ap(), 2.0 ** -6, 1.0, OP.is_ge, OP.add)
                    v.tensor_tensor(qb2.ap(), qb2.ap(), qs2.ap(), OP.mult)
                    v.tensor_scalar(qb2.ap(), qb2.ap(), C43, None, OP.mult)
                    v.tensor_tensor(qs2.ap(), q8b.ap(), qb2.ap(), OP.add)
                    v.tensor_tensor(qs2.ap(), qs2.ap(), qb2.ap(), OP.subtract)
                    v.tensor_tensor(qs2.ap(), qs2.ap(), q8b.ap(), OP.subtract)
                    ins = v.scalar_tensor_tensor(obuf, qs2.ap(), epsr.ap()[:, 0:1],
                                                 q8b.ap(), OP.mult, OP.add)
                    sq.inc(ins)
                my_sq = sq.n

                @block.scalar
                def _(s, obuf=obuf, kind=kind, ic=ic, half=half, my_sq=my_sq):
                    s.wait_ge(h_q, my_sq)
                    tgt = {"g": sc_gate, "u": sc_up, "d": sc_down}[kind]
                    ins = s.dma_start(
                        out=tgt[ic, :, half * 1024:half * 1024 + 1024], in_=obuf)
                    (sw if kind in ("g", "u") else sw2).inc(ins, 16)
                store_mark[u] = ((h_w, sw.n) if kind in ("g", "u")
                                 else (h_w2, sw2.n))
                qcur[0] += 1

        @block.sync
        def _(sync):
            for dst, src in ((qidx_t, qidx), (bQt_s, bQT), (bKt_s, bKT),
                             (gsc_s, gscT), (usc_s, uscT)):
                ins = sync.dma_start(out=dst.ap(), in_=src,
                                     allow_slow_non_contiguous=True)
                ld.inc(ins, 16)
        P0_LD = ld.n

        @block.gpsimd
        def _(g):
            ins = g.iota(pidx.ap(), [[0, 1]], channel_multiplier=1,
                         allow_small_or_imprecise_dtypes=True)
            sg_.inc(ins)

        @block.vector
        def _(v):
            v.memset(epsr.ap()[:, 1:2], 1.0)
            v.memset(ones.ap(), 1.0)
            v.memset(ident.ap(), 0.0)
            ins = v.tensor_scalar(epsr.ap()[:, 0:1], epsr.ap()[:, 1:2], EPS_HI,
                                  -EPS_HI, OP.add, OP.add)
            sv.inc(ins)

        @block.gpsimd
        def _(g):
            ins = g.iota(wacc.ap()[:, 0:128], [[1, 128]], channel_multiplier=0,
                         allow_small_or_imprecise_dtypes=True)
            sg_.inc(ins)

        @block.vector
        def _(v):
            v.wait_ge(h_g, 2)
            ins = v.tensor_scalar(ident.ap(), wacc.ap()[:, 0:128], pidx.ap(),
                                  None, OP.is_equal)
            sv.inc(ins)
            v.wait_ge(h_ld, P0_LD)
            v.tensor_tensor(bQc.ap(), bQt_s.ap()[:, 0:4], bQt_s.ap()[:, 4:8], OP.add)
            v.tensor_tensor(bQc.ap(), bQc.ap(), bQt_s.ap()[:, 8:12], OP.add)
            ins = v.tensor_tensor(bQc.ap(), bQc.ap(), bQt_s.ap()[:, 12:16], OP.add)
            sv.inc(ins)
        IDENT_V = sv.n - 1
        vmark[0] = sv.n

        def transpose_2048(src_ap_fn, dst_ap_fn, dep_v, dep_ld=None, ps=psA):
            for d0 in range(0, 16, 4):
                @block.tensor
                def _(t, d0=d0):
                    t.wait_ge(h_v, max(dep_v, IDENT_V, vmark[0]))
                    if dep_ld is not None:
                        t.wait_ge(h_ld, dep_ld)
                    for i in range(4):
                        ins = t.transpose(ps.ap()[:, i * 128:(i + 1) * 128],
                                          src_ap_fn(d0 + i), ident.ap())
                    st_.inc(ins)
                tmark[0] = st_.n

                @block.vector
                def _(v, d0=d0, tg=st_.n):
                    v.wait_ge(h_t, tg)
                    for i in range(4):
                        ins = v.tensor_copy(dst_ap_fn(d0 + i),
                                            ps.ap()[:, i * 128:(i + 1) * 128])
                    sv.inc(ins)
                vmark[0] = sv.n

        def load_transpose_W(W3, nchunks, dstW=None):
            if dstW is None:
                dstW = WT
            for n in range(nchunks):
                @block.sync
                def _(sync, n=n):
                    sync.wait_ge(h_t, tmark[0])
                    ins = sync.dma_start(out=wld.ap(), in_=W3[n])
                    ld.inc(ins, 16)
                my_ld = ld.n
                transpose_2048(
                    lambda d: wld.ap()[:, d * 128:(d + 1) * 128],
                    lambda d, n=n: dstW.ap()[:, d * 512 + n * 128: d * 512 + (n + 1) * 128],
                    dep_v=0, dep_ld=my_ld, ps=psB)

        load_transpose_W(WK3, 4)

        @block.sync
        def _(sync):
            ins = sync.dma_start(out=bVrow.ap(), in_=bV)
            ld.inc(ins, 16)
            ins = sync.dma_start(out=bKrow.ap(), in_=bK)
            ld.inc(ins, 16)
        BV_LD = ld.n
        load_transpose_W(WV3, 4, dstW=WT2)

        kv_mm_mark = {}
        for ch in range(16):
            par = ch % 2
            xt_s = xtm.ap()[:, 0:2048]
            xf_s = xfm.ap()[:, par * 2048:(par + 1) * 2048]

            @block.sync
            def _(sync, ch=ch, xt_s=xt_s):
                sync.wait_ge(h_t, tmark[0])
                ins = sync.dma_start(out=xt_s, in_=xb3[ch])
                ld.inc(ins, 16)
            my_ld = ld.n
            transpose_2048(
                lambda d, xt_s=xt_s: xt_s[:, d * 128:(d + 1) * 128],
                lambda d, xf_s=xf_s: xf_s[:, d * 128:(d + 1) * 128],
                dep_v=0, dep_ld=my_ld, ps=psA if par == 0 else psB)
            XFM_V = sv.n

            @block.tensor
            def _(t, xv=XFM_V, vm=vmark[0], xf_s=xf_s):
                t.wait_ge(h_v, max(xv, vm))
                t.wait_ge(h_ld, BV_LD)
                for g in range(4):
                    for d in range(16):
                        t.matmul(psS.ap()[:, g * 512:g * 512 + 128],
                                 WT.ap()[:, d * 512 + g * 128: d * 512 + (g + 1) * 128],
                                 xf_s[:, d * 128:(d + 1) * 128],
                                 start=(d == 0), stop=False)
                    ins = t.matmul(psS.ap()[:, g * 512:g * 512 + 128],
                                   bKrow.ap()[:, g * 128:(g + 1) * 128],
                                   ones.ap()[:, 0:128],
                                   start=False, stop=True)
                st_.inc(ins)
            tmark[0] = st_.n

            @block.vector
            def _(v, ch=ch, tg=st_.n):
                v.wait_ge(h_t, tg)
                src4 = psS.ap().rearrange("p (g rest) -> p g rest", g=4)[:, :, 0:128]
                dst4 = Kt.ap().rearrange("p (g cols) -> p g cols", g=4)[
                    :, :, ch * 128:(ch + 1) * 128]
                ins = v.tensor_copy(dst4, src4)
                sv.inc(ins)
            vmark[0] = sv.n

            @block.tensor
            def _(t, xv=XFM_V, vm=vmark[0], xf_s=xf_s, par=par):
                t.wait_ge(h_v, max(xv, vm))
                t.wait_ge(h_ld, BV_LD)
                ps = psG if par == 0 else psU
                for d in range(16):
                    t.matmul(ps.ap(),
                             xf_s[:, d * 128:(d + 1) * 128],
                             WT2.ap()[:, d * 512:(d + 1) * 512],
                             start=(d == 0), stop=False)
                ins = t.matmul(ps.ap(), ones.ap()[:, 0:128], bVrow.ap(),
                               start=False, stop=True)
                st_.inc(ins)
            tmark[0] = st_.n
            kv_mm_mark[ch] = st_.n

            @block.vector
            def _(v, ch=ch, tg=st_.n, par=par):
                v.wait_ge(h_t, tg)
                ps = psG if par == 0 else psU
                ins = v.tensor_copy(Vt.ap()[:, ch * 512:(ch + 1) * 512], ps.ap())
                sv.inc(ins)
            vmark[0] = sv.n
            quant_chunk(8)
        if PHASE_LIMIT <= 2:
            quant_chunk(NQ - qcur[0])
            return nc

        for n in range(4):
            for r in range(4):
                @block.sync
                def _(sync, n=n, r=r):
                    sync.wait_ge(h_v, vmark[0])
                    ins = sync.dma_start(out=wld.ap(), in_=WQ3[n + 4 * r])
                    ld.inc(ins, 16)
                my_ld = ld.n

                @block.vector
                def _(v, r=r, my_ld=my_ld):
                    v.wait_ge(h_ld, my_ld)
                    if r == 0:
                        ins = v.tensor_copy(wacc.ap(), wld.ap())
                    else:
                        ins = v.tensor_tensor(wacc.ap(), wacc.ap(), wld.ap(), OP.add)
                    sv.inc(ins)
                vmark[0] = sv.n
            transpose_2048(
                lambda d: wacc.ap()[:, d * 128:(d + 1) * 128],
                lambda d, n=n: WT.ap()[:, d * 512 + n * 128: d * 512 + (n + 1) * 128],
                dep_v=vmark[0], ps=psB)
        akv.free()
        asc.allocate()

        for j in range(NSLOT):
            @block.sync
            def _(sync, j=j):
                sync.wait_ge(h_t, tmark[0])
                ins = sync.dma_start(out=xq1.ap(), in_=xq3[j])
                ld.inc(ins, 16)
            my_ld = ld.n
            transpose_2048(
                lambda d: xq1.ap()[:, d * 128:(d + 1) * 128],
                lambda d: xqf.ap()[:, d * 128:(d + 1) * 128],
                dep_v=0, dep_ld=my_ld, ps=psA)
            XQF_V = sv.n
            for g in range(4):
                @block.tensor
                def _(t, g=g, xv=XQF_V, vm=vmark[0]):
                    t.wait_ge(h_v, max(xv, vm))
                    for d in range(16):
                        ins = t.matmul(psB.ap()[:, 0:128],
                                       WT.ap()[:, d * 512 + g * 128: d * 512 + (g + 1) * 128],
                                       xqf.ap()[:, d * 128:(d + 1) * 128],
                                       start=(d == 0), stop=(d == 15))
                    st_.inc(ins)
                tmark[0] = st_.n

                @block.vector
                def _(v, g=g, j=j, tg=st_.n):
                    v.wait_ge(h_t, tg)
                    ins = v.tensor_scalar(
                        Qt.ap()[:, g * 1024 + j * 128: g * 1024 + (j + 1) * 128],
                        psB.ap()[:, 0:128], bQc.ap()[:, g:g + 1], None, OP.add)
                    sv.inc(ins)
                vmark[0] = sv.n
            quant_chunk(3)
        KVQ_V = sv.n
        if PHASE_LIMIT <= 3:
            quant_chunk(NQ - qcur[0])
            return nc

        for j in range(NSLOT):
            cj = CJ[j]
            kv = cj * 512

            @block.gpsimd
            def _(g, j=j, cj=cj):
                g.wait_ge(h_v, vmark[0])
                ins = g.iota(miota.ap(), [[1, 512]], base=(cj - 1) * 512,
                             channel_multiplier=0,
                             allow_small_or_imprecise_dtypes=True)
                sg_.inc(ins)
            IOTA_G = sg_.n

            @block.tensor
            def _(t, j=j, cj=cj):
                t.wait_ge(h_v, KVQ_V)
                t.wait_ge(h_x, sx.n)
                for cc in range(cj):
                    for g in range(4):
                        ins = t.matmul(
                            psS.ap()[:, cc * 512:(cc + 1) * 512],
                            Qt.ap()[:, g * 1024 + j * 128: g * 1024 + (j + 1) * 128],
                            Kt.ap()[:, g * 2048 + cc * 512: g * 2048 + (cc + 1) * 512],
                            start=(g == 0), stop=(g == 3))
                st_.inc(ins)
            S_T = st_.n

            @block.vector
            def _(v, j=j, cj=cj, kv=kv, tg=S_T, ig=IOTA_G):
                v.wait_ge(h_g, ig)
                v.tensor_scalar(mask.ap(), miota.ap(), qidx_t.ap()[:, j:j + 1],
                                NEG_BIG, OP.is_gt, OP.mult)
                v.wait_ge(h_t, tg)
                v.tensor_tensor(psS.ap()[:, (cj - 1) * 512: cj * 512],
                                psS.ap()[:, (cj - 1) * 512: cj * 512],
                                mask.ap(), OP.add)
                ins = v.reduce_max(red.ap()[:, 0:1], psS.ap()[:, 0:kv],
                                   axis=AX, negate=True)
                sv.inc(ins)
            MASK_V = sv.n
            vmark[0] = sv.n

            @block.scalar
            def _(s, kv=kv, mv=MASK_V):
                s.wait_ge(h_v, mv)
                ins = s.activation(P.ap()[:, 0:kv], psS.ap()[:, 0:kv], AF.Exp,
                                   bias=red.ap()[:, 0:1], scale=1.0,
                                   accum_out=red.ap()[:, 1:2])
                sx.inc(ins)
            EXP_X = sx.n

            @block.vector
            def _(v, ea=EXP_X):
                v.wait_ge(h_x, ea)
                ins = v.reciprocal(red.ap()[:, 2:3], red.ap()[:, 1:2])
                sv.inc(ins)
            RCP_V = sv.n

            nkb = 4 * cj
            for b0 in range(0, nkb, 4):
                nb = min(4, nkb - b0)

                @block.tensor
                def _(t, b0=b0, nb=nb, ea=EXP_X):
                    t.wait_ge(h_x, ea)
                    t.wait_ge(h_v, vmark[0])
                    for i in range(nb):
                        ins = t.transpose(psA.ap()[:, i * 128:(i + 1) * 128],
                                          P.ap()[:, (b0 + i) * 128:(b0 + i + 1) * 128],
                                          ident.ap())
                    st_.inc(ins)

                @block.vector
                def _(v, b0=b0, nb=nb, tg=st_.n):
                    v.wait_ge(h_t, tg)
                    ins = v.tensor_copy(PT.ap()[:, b0 * 128:(b0 + nb) * 128],
                                        psA.ap()[:, 0:nb * 128])
                    sv.inc(ins)
                vmark[0] = sv.n
            PT_V = sv.n

            @block.tensor
            def _(t, nkb=nkb, pv=PT_V):
                t.wait_ge(h_v, pv)
                for kb in range(nkb):
                    ins = t.matmul(psB.ap(), PT.ap()[:, kb * 128:(kb + 1) * 128],
                                   Vt.ap()[:, kb * 512:(kb + 1) * 512],
                                   start=(kb == 0), stop=(kb == nkb - 1))
                st_.inc(ins)
            Z_T = st_.n

            @block.vector
            def _(v, tg=Z_T, rv=RCP_V):
                v.wait_ge(h_t, tg)
                ins = v.tensor_scalar(Ztm.ap(), psB.ap(), red.ap()[:, 2:3],
                                      None, OP.mult)
                sv.inc(ins)
            ZTM_V = sv.n
            vmark[0] = sv.n

            @block.tensor
            def _(t, zv=ZTM_V):
                t.wait_ge(h_v, zv)
                for i in range(4):
                    ins = t.transpose(psA.ap()[:, i * 128:(i + 1) * 128],
                                      Ztm.ap()[:, i * 128:(i + 1) * 128],
                                      ident.ap())
                st_.inc(ins)

            @block.vector
            def _(v, j=j, tg=st_.n):
                v.wait_ge(h_t, tg)
                for i in range(4):
                    ins = v.tensor_copy(
                        Zf.ap()[:, i * 1024 + j * 128: i * 1024 + (j + 1) * 128],
                        psA.ap()[:, i * 128:(i + 1) * 128])
                sv.inc(ins)
            vmark[0] = sv.n
            quant_chunk(5)
        QDONE_W = 16 * 192
        if PHASE_LIMIT <= 4:
            return nc
        asc.free()
        ab.free()
        af.allocate()
        af2.allocate(side="right")

        for g in range(4):
            for r in range(4):
                @block.sync
                def _(sync, g=g, r=r):
                    sync.wait_ge(h_v, vmark[0])
                    ins = sync.dma_start(out=otile.ap()[:, 0:2048], in_=Wp3[g + 4 * r])
                    ld.inc(ins, 16)
                my_ld = ld.n

                @block.vector
                def _(v, g=g, r=r, my_ld=my_ld):
                    v.wait_ge(h_ld, my_ld)
                    dst = projc.ap()[:, g * 2048:(g + 1) * 2048]
                    pl = otile.ap()[:, 0:2048]
                    if r == 0:
                        ins = v.tensor_copy(dst, pl)
                    elif r < 3:
                        ins = v.tensor_tensor(dst, dst, pl, OP.add)
                    else:
                        v.tensor_tensor(dst, dst, pl, OP.add)
                        ins = v.tensor_scalar(dst, dst, INV_SQRT_HS, None, OP.mult)
                    sv.inc(ins)
                vmark[0] = sv.n
        PROJC_V = sv.n
        if PHASE_LIMIT <= 5:
            return nc

        @block.sync
        def _(sync):
            ins = sync.dma_start(out=dscB.ap()[0:1, :], in_=dsc)
            ld.inc(ins, 16)
        DSC_LD = ld.n

        @block.tensor
        def _(t):
            t.wait_ge(h_v, PROJC_V)
            t.wait_ge(h_ld, DSC_LD)
            for cc, ps in enumerate((psA, psB, psG, psU)):
                ins = t.matmul(ps.ap(), ones.ap()[:, 0:128],
                               dscB.ap()[0:1, cc * 512:(cc + 1) * 512],
                               start=True, stop=True)
            st_.inc(ins)

        @block.vector
        def _(v, base_t=st_.n):
            v.wait_ge(h_t, base_t)
            for cc, ps in enumerate((psA, psB, psG, psU)):
                ins = v.tensor_copy(dscB.ap()[:, cc * 512:(cc + 1) * 512], ps.ap())
                sv.inc(ins)
        vmark[0] = sv.n
        tmark[0] = st_.n
        psfree = [sv.n]
        if PHASE_LIMIT <= 5.5:
            return nc

        for c in range(NFC):
            if PHASE_LIMIT <= 5.6 and c > 0:
                return nc
            @block.sync
            def _(sync, c=c):
                sync.wait_ge(h_v, vmark[0])
                for jj in range(2):
                    ins = sync.dma_start(out=otile.ap()[:, jj * 2048:(jj + 1) * 2048],
                                         in_=xq3[2 * c + jj])
                    ld.inc(ins, 16)
            XQ_LD = ld.n

            ao_marks = []
            ao_gate = [0]
            for jj in range(2):
                @block.tensor
                def _(t, c=c, jj=jj, pf=None):
                    t.wait_ge(h_v, max(PROJC_V, psfree[0] if c > 0 else vmark[0],
                                       ao_gate[0]))
                    for dd in range(4):
                        for g in range(4):
                            ins = t.matmul(
                                psS.ap()[:, dd * 512:(dd + 1) * 512],
                                Zf.ap()[:, g * 1024 + (2 * c + jj) * 128:
                                        g * 1024 + (2 * c + jj + 1) * 128],
                                projc.ap()[:, g * 2048 + dd * 512:
                                           g * 2048 + (dd + 1) * 512],
                                start=(g == 0), stop=(g == 3))
                    st_.inc(ins)
                tmark[0] = st_.n

                @block.vector
                def _(v, jj=jj, tg=st_.n, xl_=XQ_LD, oo=so.n):
                    v.wait_ge(h_t, tg)
                    v.wait_ge(h_ld, xl_)
                    v.wait_ge(h_o, oo)
                    ins = v.scalar_tensor_tensor(
                        rtile.ap()[:, jj * 2048:(jj + 1) * 2048],
                        psS.ap(), 1.0, otile.ap()[:, jj * 2048:(jj + 1) * 2048],
                        OP.mult, OP.add,
                        accum_out=red.ap()[:, 32 + jj * 8:33 + jj * 8])
                    sv.inc(ins)
                ao_marks.append(sv.n)
                ao_gate[0] = sv.n
                vmark[0] = sv.n

            @block.vector
            def _(v):
                for jj in range(2):
                    v.tensor_scalar(red.ap()[:, 33 + jj * 8:34 + jj * 8],
                                    red.ap()[:, 32 + jj * 8:33 + jj * 8],
                                    INV_D, None, OP.mult)
                for jj in range(2):
                    v.tensor_scalar(otile.ap()[:, jj * 2048:(jj + 1) * 2048],
                                    rtile.ap()[:, jj * 2048:(jj + 1) * 2048],
                                    red.ap()[:, 33 + jj * 8:34 + jj * 8],
                                    None, OP.subtract)
                for jj in range(2):
                    v.scalar_tensor_tensor(
                        rtile.ap()[:, jj * 2048:(jj + 1) * 2048],
                        otile.ap()[:, jj * 2048:(jj + 1) * 2048], 1.0,
                        otile.ap()[:, jj * 2048:(jj + 1) * 2048], OP.mult, OP.mult,
                        accum_out=red.ap()[:, 34 + jj * 8:35 + jj * 8])
                for jj in range(2):
                    ins = v.tensor_scalar(red.ap()[:, 35 + jj * 8:36 + jj * 8],
                                          red.ap()[:, 34 + jj * 8:35 + jj * 8],
                                          INV_D, LN_EPS, OP.mult, OP.add)
                sv.inc(ins)
            VAR_V = sv.n

            @block.scalar
            def _(s, vv=VAR_V):
                s.wait_ge(h_v, vv)
                for jj in range(2):
                    ins = s.activation(red.ap()[:, 36 + jj * 8:37 + jj * 8],
                                       red.ap()[:, 35 + jj * 8:36 + jj * 8], AF.Sqrt)
                sa.inc(ins)
            SQ_A = sa.n

            @block.vector
            def _(v, qa=SQ_A):
                v.wait_ge(h_a, qa)
                for jj in range(2):
                    v.reciprocal(red.ap()[:, 37 + jj * 8:38 + jj * 8],
                                 red.ap()[:, 36 + jj * 8:37 + jj * 8])
                for jj in range(2):
                    v.tensor_tensor(red.ap()[:, 38 + jj * 8:39 + jj * 8],
                                    red.ap()[:, 37 + jj * 8:38 + jj * 8],
                                    red.ap()[:, 37 + jj * 8:38 + jj * 8], OP.mult)
                for jj in range(2):
                    v.tensor_tensor(red.ap()[:, 38 + jj * 8:39 + jj * 8],
                                    red.ap()[:, 38 + jj * 8:39 + jj * 8],
                                    red.ap()[:, 35 + jj * 8:36 + jj * 8], OP.mult)
                for jj in range(2):
                    v.tensor_scalar(red.ap()[:, 38 + jj * 8:39 + jj * 8],
                                    red.ap()[:, 38 + jj * 8:39 + jj * 8],
                                    -0.5, 1.5, OP.mult, OP.add)
                for jj in range(2):
                    v.tensor_tensor(red.ap()[:, 37 + jj * 8:38 + jj * 8],
                                    red.ap()[:, 37 + jj * 8:38 + jj * 8],
                                    red.ap()[:, 38 + jj * 8:39 + jj * 8], OP.mult)
                for jj in range(2):
                    ins = v.tensor_scalar(
                        x1tm.ap()[:, jj * 2048:(jj + 1) * 2048],
                        otile.ap()[:, jj * 2048:(jj + 1) * 2048],
                        red.ap()[:, 37 + jj * 8:38 + jj * 8], None, OP.mult)
                    sv.inc(ins)
            X1_V = sv.n
            vmark[0] = sv.n
            if PHASE_LIMIT <= 6:
                return nc

            for jj in range(2):
                for d0 in range(0, 16, 4):
                    @block.tensor
                    def _(t, jj=jj, d0=d0, xv=X1_V):
                        t.wait_ge(h_v, max(xv, vmark[0]))
                        for i in range(4):
                            ins = t.transpose(
                                psA.ap()[:, i * 128:(i + 1) * 128],
                                x1tm.ap()[:, jj * 2048 + (d0 + i) * 128:
                                          jj * 2048 + (d0 + i + 1) * 128],
                                ident.ap())
                        st_.inc(ins)
                    tmark[0] = st_.n

                    @block.vector
                    def _(v, jj=jj, d0=d0, tg=st_.n):
                        v.wait_ge(h_t, tg)
                        for i in range(4):
                            dsl = slice((d0 + i) * 256 + jj * 128,
                                        (d0 + i) * 256 + (jj + 1) * 128)
                            v.tensor_copy(xh.ap()[:, dsl],
                                          psA.ap()[:, i * 128:(i + 1) * 128])
                            ins = v.scalar_tensor_tensor(
                                xl.ap()[:, dsl], psA.ap()[:, i * 128:(i + 1) * 128],
                                1.0, xh.ap()[:, dsl], OP.mult, OP.subtract)
                        sv.inc(ins)
                    vmark[0] = sv.n
            XHL_V = sv.n
            if PHASE_LIMIT <= 6.5:
                return nc

            ic_te_mark = {}
            ic_hl_mark = {}
            for ic in range(NIC):
                s2 = ic % 2
                gbuf = fwq.ap()[:, (2 * s2) * 2048:(2 * s2 + 1) * 2048]
                ubuf = fwq.ap()[:, (2 * s2 + 1) * 2048:(2 * s2 + 2) * 2048]

                gslot, uslot = 2 * s2, 2 * s2 + 1

                @block.sync
                def _(sync, c=c, ic=ic, gbuf=gbuf, ubuf=ubuf, gs=gslot, us=uslot):
                    if ic >= 2:
                        sync.wait_ge(h_t, ic_te_mark[("g", ic - 2)])
                    sync.wait_ge(h_w, min(16 * (4 * ic + 4), QDONE_W))
                    ins = sync.dma_start(out=gbuf[:, 0:2048], in_=sc_gate[ic])
                    sf[gs].inc(ins, 16)
                    if ic >= 2:
                        sync.wait_ge(h_t, ic_te_mark[("u", ic - 2)])
                    ins = sync.dma_start(out=ubuf[:, 0:2048], in_=sc_up[ic])
                    sf[us].inc(ins, 16)
                GLS, GLC = sf[gslot].h, sf[gslot].n
                ULS, ULC = sf[uslot].h, sf[uslot].n

                @block.tensor
                def _(t, ic=ic, gbuf=gbuf, gls=GLS, glc=GLC, xv=XHL_V):
                    t.wait_ge(h_v, xv)
                    t.wait_ge(gls, glc)
                    if ic >= 1:
                        t.wait_ge(h_a, ic_te_mark.get(("silu", ic - 1), 0))
                    for d in range(16):
                        for hsel, rhs in ((0, xh), (1, xl)):
                            ins = t.matmul(psG.ap()[:, 0:FFN_TC],
                                           gbuf[:, d * 128:(d + 1) * 128],
                                           rhs.ap()[:, d * FFN_TC:(d + 1) * FFN_TC],
                                           start=(d == 0 and hsel == 0),
                                           stop=(d == 15 and hsel == 1))
                    st_.inc(ins)
                G_T = st_.n
                ic_te_mark[("g", ic)] = G_T

                @block.scalar
                def _(s, ic=ic, tg=G_T, pv=ic_hl_mark.get(ic - 1, 0)):
                    s.wait_ge(h_t, tg)
                    s.wait_ge(h_v, pv)
                    ins = s.activation(Zf.ap()[:, 0:FFN_TC], psG.ap()[:, 0:FFN_TC],
                                       AF.Silu, bias=0.0,
                                       scale=gsc_s.ap()[:, ic:ic + 1])
                    sa.inc(ins)
                ic_te_mark[("silu", ic)] = sa.n
                SIL_A = sa.n

                @block.tensor
                def _(t, ic=ic, ubuf=ubuf, uls=ULS, ulc=ULC,
                      hv=ic_hl_mark.get(ic - 1, 0)):
                    t.wait_ge(uls, ulc)
                    t.wait_ge(h_v, hv)
                    for d in range(16):
                        for hsel, rhs in ((0, xh), (1, xl)):
                            ins = t.matmul(psU.ap()[:, 0:FFN_TC],
                                           ubuf[:, d * 128:(d + 1) * 128],
                                           rhs.ap()[:, d * FFN_TC:(d + 1) * FFN_TC],
                                           start=(d == 0 and hsel == 0),
                                           stop=(d == 15 and hsel == 1))
                    st_.inc(ins)
                ic_te_mark[("u", ic)] = st_.n
                U_T = st_.n
                tmark[0] = st_.n
                quant_chunk(2)

                @block.vector
                def _(v, ic=ic, ut=U_T, sa_=SIL_A):
                    v.wait_ge(h_t, ut)
                    v.wait_ge(h_a, sa_)
                    v.scalar_tensor_tensor(psB.ap()[:, 0:FFN_TC], psU.ap()[:, 0:FFN_TC],
                                           usc_s.ap()[:, ic:ic + 1], Zf.ap()[:, 0:FFN_TC],
                                           OP.mult, OP.mult)
                    v.tensor_copy(hh.ap()[:, ic * FFN_TC:(ic + 1) * FFN_TC],
                                  psB.ap()[:, 0:FFN_TC])
                    ins = v.scalar_tensor_tensor(
                        hl.ap()[:, ic * FFN_TC:(ic + 1) * FFN_TC], psB.ap()[:, 0:FFN_TC],
                        1.0, hh.ap()[:, ic * FFN_TC:(ic + 1) * FFN_TC], OP.mult, OP.subtract)
                    sv.inc(ins)
                ic_hl_mark[ic] = sv.n
                vmark[0] = sv.n
            HDONE_V = sv.n
            if PHASE_LIMIT <= 7:
                return nc
            if PHASE_LIMIT <= 7.2 and c == 0:
                return nc

            dbanks = [[(psS, cc * 512) for cc in range(4)],
                      [(psG, 0), (psU, 0), (psA, 0), (psB, 0)]]
            dn_te = {}
            for ic in range(NIC):
                s2 = ic % 4
                buf = fwq.ap()[:, s2 * 2048:(s2 + 1) * 2048]

                @block.sync
                def _(sync, c=c, ic=ic, buf=buf, ds=s2):
                    if ic < 4:
                        sync.wait_ge(h_t, tmark[0])
                    else:
                        sync.wait_ge(h_t, dn_te[ic - 4])
                    sync.wait_ge(h_w2, 16 * 96)
                    ins = sync.dma_start(out=buf[:, 0:2048], in_=sc_down[ic])
                    sf[ds].inc(ins, 16)
                DLS, DLC = sf[s2].h, sf[s2].n

                @block.tensor
                def _(t, ic=ic, buf=buf, dls=DLS, dlc=DLC,
                      hv=(HDONE_V if ic == 0 else ic_hl_mark[ic]),
                      la=(ic_te_mark[("silu", NIC - 1)] if ic == 0 else 0)):
                    t.wait_ge(dls, dlc)
                    t.wait_ge(h_v, hv)
                    if la:
                        t.wait_ge(h_a, la)
                    for s in range(2):
                        for hsel, lhs in ((0, hh), (1, hl)):
                            for cc in range(4):
                                if s == 0:
                                    tgt = psS.ap()[:, cc * 512:(cc + 1) * 512]
                                else:
                                    tgt = dbanks[1][cc][0].ap()
                                ins = t.matmul(
                                    tgt,
                                    lhs.ap()[:, ic * FFN_TC + s * 128:
                                             ic * FFN_TC + (s + 1) * 128],
                                    buf[:, cc * 512:(cc + 1) * 512],
                                    start=(ic == 0 and hsel == 0),
                                    stop=(ic == NIC - 1 and hsel == 1))
                    st_.inc(ins)
                dn_te[ic] = st_.n
                tmark[0] = st_.n
            DN_T = st_.n
            if PHASE_LIMIT <= 7.5:
                return nc

            @block.vector
            def _(v, c=c, tg=DN_T):
                v.wait_ge(h_t, tg)
                for s in range(2):
                    for cc in range(4):
                        ps, off = dbanks[s][cc]
                        src = (psS.ap()[:, cc * 512:(cc + 1) * 512] if s == 0
                               else ps.ap())
                        ins = v.tensor_tensor(
                            otile.ap()[:, s * 2048 + cc * 512: s * 2048 + (cc + 1) * 512],
                            src, dscB.ap()[:, cc * 512:(cc + 1) * 512], OP.mult)
                sv.inc(ins)
            psfree[0] = sv.n

            @block.vector
            def _(v, c=c):
                for s in range(2):
                    v.scalar_tensor_tensor(
                        rtile.ap()[:, s * 2048:(s + 1) * 2048],
                        otile.ap()[:, s * 2048:(s + 1) * 2048], 1.0,
                        x1tm.ap()[:, s * 2048:(s + 1) * 2048], OP.mult, OP.add,
                        accum_out=red.ap()[:, 48 + s * 8:49 + s * 8])
                for s in range(2):
                    v.tensor_scalar(red.ap()[:, 49 + s * 8:50 + s * 8],
                                    red.ap()[:, 48 + s * 8:49 + s * 8],
                                    INV_D, None, OP.mult)
                for s in range(2):
                    v.tensor_scalar(otile.ap()[:, s * 2048:(s + 1) * 2048],
                                    rtile.ap()[:, s * 2048:(s + 1) * 2048],
                                    red.ap()[:, 49 + s * 8:50 + s * 8],
                                    None, OP.subtract)
                for s in range(2):
                    v.scalar_tensor_tensor(
                        rtile.ap()[:, s * 2048:(s + 1) * 2048],
                        otile.ap()[:, s * 2048:(s + 1) * 2048], 1.0,
                        otile.ap()[:, s * 2048:(s + 1) * 2048], OP.mult, OP.mult,
                        accum_out=red.ap()[:, 50 + s * 8:51 + s * 8])
                for s in range(2):
                    ins = v.tensor_scalar(red.ap()[:, 51 + s * 8:52 + s * 8],
                                          red.ap()[:, 50 + s * 8:51 + s * 8],
                                          INV_D, LN_EPS, OP.mult, OP.add)
                sv.inc(ins)
            VAR2_V = sv.n

            @block.scalar
            def _(s_, vv=VAR2_V):
                s_.wait_ge(h_v, vv)
                for s in range(2):
                    ins = s_.activation(red.ap()[:, 52 + s * 8:53 + s * 8],
                                        red.ap()[:, 51 + s * 8:52 + s * 8], AF.Sqrt)
                sa.inc(ins)
            SQ2_A = sa.n

            @block.vector
            def _(v, qa=SQ2_A):
                v.wait_ge(h_a, qa)
                for s in range(2):
                    v.reciprocal(red.ap()[:, 53 + s * 8:54 + s * 8],
                                 red.ap()[:, 52 + s * 8:53 + s * 8])
                for s in range(2):
                    v.tensor_tensor(red.ap()[:, 54 + s * 8:55 + s * 8],
                                    red.ap()[:, 53 + s * 8:54 + s * 8],
                                    red.ap()[:, 53 + s * 8:54 + s * 8], OP.mult)
                for s in range(2):
                    v.tensor_tensor(red.ap()[:, 54 + s * 8:55 + s * 8],
                                    red.ap()[:, 54 + s * 8:55 + s * 8],
                                    red.ap()[:, 51 + s * 8:52 + s * 8], OP.mult)
                for s in range(2):
                    v.tensor_scalar(red.ap()[:, 54 + s * 8:55 + s * 8],
                                    red.ap()[:, 54 + s * 8:55 + s * 8],
                                    -0.5, 1.5, OP.mult, OP.add)
                for s in range(2):
                    v.tensor_tensor(red.ap()[:, 53 + s * 8:54 + s * 8],
                                    red.ap()[:, 53 + s * 8:54 + s * 8],
                                    red.ap()[:, 54 + s * 8:55 + s * 8], OP.mult)
                for s in range(2):
                    ins = v.tensor_scalar(
                        rtile.ap()[:, s * 2048:(s + 1) * 2048],
                        otile.ap()[:, s * 2048:(s + 1) * 2048],
                        red.ap()[:, 53 + s * 8:54 + s * 8], None, OP.mult)
                    sv.inc(ins)
            OUT_V = sv.n
            vmark[0] = sv.n

            @block.scalar
            def _(s, c=c, ov=OUT_V):
                s.wait_ge(h_v, ov)
                for jj in range(2):
                    ins = s.dma_start(out=out3[2 * c + jj],
                                      in_=rtile.ap()[:, jj * 2048:(jj + 1) * 2048])
                    so.inc(ins, 16)

        af.free()
        af2.free()
        abq.free()
        ag.free()

        @block.sync
        def _(sync):
            sync.wait_ge(h_o, so.n)
    return nc


_PROGRAM_CACHE = {}


def kernel(L0_attn_x, L0_attn_WQ, L0_attn_bQ, L0_attn_WK, L0_attn_bK,
           L0_attn_WV, L0_attn_bV, L0_attn_proj, L0_ffn_gate_w, L0_ffn_up_w,
           L0_ffn_down_w, L0_ffn_gate_scale, L0_ffn_up_scale,
           L0_ffn_down_scale):
    x = np.ascontiguousarray(np.asarray(L0_attn_x, dtype=np.float32))
    f32 = lambda a: np.ascontiguousarray(np.asarray(a, dtype=np.float32))
    WQv, bQv = f32(L0_attn_WQ), f32(L0_attn_bQ).reshape(1, D)
    WKv, bKv = f32(L0_attn_WK), f32(L0_attn_bK).reshape(1, DKV)
    WVv, bVv = f32(L0_attn_WV), f32(L0_attn_bV).reshape(1, DKV)
    projv = f32(L0_attn_proj)
    gwv, uwv, dwv = f32(L0_ffn_gate_w), f32(L0_ffn_up_w), f32(L0_ffn_down_w)
    gsv = f32(L0_ffn_gate_scale).reshape(1, DI)
    usv = f32(L0_ffn_up_scale).reshape(1, DI)
    dsv = f32(L0_ffn_down_scale).reshape(1, D)

    if "nc" not in _PROGRAM_CACHE:
        _PROGRAM_CACHE["nc"] = build_program()
    nc = _PROGRAM_CACHE["nc"]

    in_maps = []
    p = np.arange(128, dtype=np.float32)
    for c in range(8):
        b, h = c // 2, c % 2
        blocks = [2 * j + h for j in range(NSLOT)]
        xqv = np.ascontiguousarray(
            x[b].reshape(16, 128, D)[blocks].reshape(TOK, D))
        qidxv = np.stack([(2 * j + h) * 128 + p for j in range(NSLOT)],
                         axis=1).astype(np.float32)
        in_maps.append({
            "xb": np.ascontiguousarray(x[b]), "xq": xqv, "qidx": qidxv,
            "WQ": WQv, "bQ": bQv, "WK": WKv, "bK": bKv, "WV": WVv, "bV": bVv,
            "proj": projv, "gate_w": gwv, "up_w": uwv, "down_w": dwv,
            "gate_scale": gsv, "up_scale": usv, "down_scale": dsv,
        })
    res = run_bass_kernel_spmd(nc, in_maps, core_ids=list(range(8)))
    y = np.empty((B, T, D), dtype=np.float32)
    for c in range(8):
        b, h = c // 2, c % 2
        o = res.results[c]["out"].reshape(NSLOT, 128, D)
        for j in range(NSLOT):
            y[b, (2 * j + h) * 128:(2 * j + h + 1) * 128, :] = o[j]
    return y
